# version 16
# baseline (speedup 1.0000x reference)
import numpy as np
import concourse.bass as bass
import concourse.mybir as mybir
from concourse.bass_utils import run_bass_kernel_spmd

F32 = mybir.dt.float32
BF16 = mybir.dt.bfloat16
AF = mybir.ActivationFunctionType
ALU = mybir.AluOpType
AX = mybir.AxisListType

S = 2048
D = 1024
H = 16
FF = 4096
NCH = S // 128
FCH = D // 128
EPS = 1e-5

ENGS = ("pe", "act", "dve", "pool", "sp")


class Sem:
    def __init__(self, h):
        self.h = h
        self.n = 0


class B:

    def __init__(self):
        self.q = {e: [] for e in ENGS}
        self.hw = {e: {} for e in ENGS}
        self.meta = {e: [] for e in ENGS}

    def wait(self, eng, sem, val):
        if val is None:
            return
        if callable(val):
            self.q[eng].append(lambda e, s=sem.h, v=val: e.wait_ge(s, v()))
            self.meta[eng].append(((sem, val), None, 0))
            return
        if val <= 0:
            return
        if self.hw[eng].get(id(sem), -1) >= val:
            return
        self.hw[eng][id(sem)] = val
        self.q[eng].append(lambda e, s=sem.h, v=val: e.wait_ge(s, v))
        self.meta[eng].append(((sem, val), None, 0))

    def op(self, eng, make, waits=(), inc=None, inc_k=1):
        for w in waits:
            if w is None:
                continue
            sem, val = w
            self.wait(eng, sem, val)
        if inc is None:
            self.q[eng].append(make)
            self.meta[eng].append((None, None, 0))
            return None
        inc.n += inc_k
        self.q[eng].append(
            lambda e, m=make, s=inc.h, k=inc_k: m(e).then_inc(s, k))
        self.meta[eng].append((None, inc, inc_k))
        return inc.n

    def barrier(self, bar, idx):
        for e in ENGS:
            bar.n += 1
            self.q[e].append(lambda en, s=bar.h: en.sem_inc(s, 1))
            self.meta[e].append((None, bar, 1))
        for e in ENGS:
            self.wait(e, bar, 5 * idx)

    def check_deadlock(self):
        pos = {e: 0 for e in ENGS}
        cnt = {}
        while True:
            progress = False
            for e in ENGS:
                while pos[e] < len(self.meta[e]):
                    w, inc, k = self.meta[e][pos[e]]
                    if w is not None:
                        sem, val = w
                        if callable(val):
                            val = val()
                        if val is not None and val > 0 and \
                                cnt.get(id(sem), 0) < val:
                            break
                    if inc is not None:
                        cnt[id(inc)] = cnt.get(id(inc), 0) + k
                    pos[e] += 1
                    progress = True
            if not progress:
                break
        stuck = {e: (pos[e], len(self.meta[e])) for e in ENGS
                 if pos[e] < len(self.meta[e])}
        return stuck


def build_nc(debug=()):
    nc = bass.Bass("TRN2", target_bir_lowering=False, debug=False)
    debug = set(debug)

    din = lambda n, s: nc.dram_tensor(n, s, F32, kind="ExternalInput").ap()
    x_d = din("x", [S, D])
    wq_d = din("wq", [8, 128, 1024])
    wk_d = din("wk", [8, 128, 1024])
    wo_d = din("wo", [8, 128, 1024])
    wv_d = din("wv", [8, 128, 1024])
    w1_d = din("w1", [32, 128, 1024])
    w2_d = din("w2", [32, 128, 1024])
    h1_d = din("h1", [32, 128, 4096])
    h2_d = din("h2", [32, 128, 4096])
    w3_d = din("w3", [8, 128, 4096])
    bq_d = din("bq", [128, 8])
    bk_d = din("bk", [128, 8])
    bo_d = din("bo", [128, 8])
    bv_d = din("bv", [1, D])
    sl_d = din("sl", [1, H])
    s1_d = din("s1", [128, 32])
    s2_d = din("s2", [128, 32])
    sh1_d = din("sh1", [128, 32])
    sh2_d = din("sh2", [128, 32])
    s3_d = din("s3", [128, 8])
    out_d = nc.dram_tensor("out", [S, D], F32, kind="ExternalOutput").ap()
    x1_d = nc.dram_tensor("x1d", [S, D], F32, kind="Internal").ap()
    dbg = {}
    for name in debug:
        shp = {"ln1": [128, FCH * S], "q": [128, FCH * S], "k": [128, FCH * S],
               "v": [128, NCH * H * 65], "ctx": [128, NCH * D],
               "att": [128, FCH * S], "ln2": [128, FCH * S],
               "h": [128, 32 * 1024], "g": [128, 32 * 1024],
               "ffn": [128, 8 * 1024], "etm0": [128, 1024],
               "x1r0": [128, 1024]}[name]
        dbg[name] = nc.dram_tensor("dbg_" + name, shp, F32,
                                   kind="ExternalOutput").ap()

    SB0 = (nc.sbuf_base + 31) // 32 * 32
    def at(name, off, shape, dtype):
        return nc.alloc_sbuf_tensor_at(name, list(shape), dtype,
                                       offset=SB0 + off).ap()

    ln1 = at("ln1", 0, [128, FCH, S], BF16)
    q_fm = at("q_fm", 32768, [128, FCH, S], BF16)
    k_fm = at("k_fm", 65536, [128, FCH, S], BF16)
    v_tm = at("v_tm", 98304, [128, NCH, H * 65], BF16)
    ctx_tm = at("ctx_tm", 131584, [128, NCH, D], BF16)
    ln2 = at("ln2", 131584, [128, FCH, S], BF16)
    h_buf = at("h_buf", 0, [128, 32, 1024], BF16)
    g_buf = at("g_buf", 65536, [128, 32, 1024], BF16)
    ctx_fm = at("ctx_fm", 32768, [128, FCH, S], BF16)
    att_fm = at("att_fm", 65536, [128, FCH, S], BF16)
    ffn_fm = at("ffn_fm", 65536, [128, 8, 1024], BF16)
    ffn_tm = at("ffn_tm", 81920, [128, 3, 1024], BF16)
    x1rel = at("x1rel", 88064, [128, 3, 1024], F32)
    sq_s = at("sq_s", 100352, [128, 2, 1024], F32)
    ln4t = at("ln4t", 108544, [128, 2, 1024], F32)
    a3tm = at("a3tm", 98304, [128, 4, 1024], BF16)
    a3sq = at("a3sq", 106496, [128, 4, 1024], F32)
    a3ln = at("a3ln", 122880, [128, 2, 1024], F32)
    wv_sb = at("wv_sb", 164352, [128, FCH, D], BF16)
    p0buf = at("p0buf", 164352, [128, 2, 1024], BF16)
    ctxu = at("ctxu", 168448, [128, 2, 1024], BF16)
    q_z = at("q_z", 172544, [128, 2, 2048], BF16)
    wbuf = at("wbuf", 164352, [128, 2, 4096], BF16)
    wstage = at("wstage", 180736, [128, 2, 1024], BF16)
    sA = at("sA", 180736, [128, 1024], F32)
    rowbf = at("rowbf", 182784, [1, 1024], BF16)
    xbuf = at("xbuf", 184832, [128, 3, 1024], F32)
    mrow = at("mrow", 193024, [128, 1024], F32)
    mc = at("mc", 197120, [128, 640], F32)
    bvhl = at("bvhl", 199680, [2, 1024], BF16)
    ones2 = at("ones2", 201728, [2, 128], BF16)
    I128 = at("I128", 202016, [128, 128], BF16)
    iotf = at("iotf", 202272, [128, 256], F32)
    brow = at("brow", 203296, [1, 2048], F32)
    a0tm = at("a0tm", 131584, [128, 3, 1024], BF16)
    a0sq = at("a0sq", 137728, [128, 2, 1024], F32)
    C_BQ, C_BK, C_BO, C_W3S = 0, 8, 16, 24
    C_W1S, C_W2S, C_H1S, C_H2S = 32, 64, 96, 128
    C_SL, C_BIAS, C_ST = 160, 176, 432

    from contextlib import ExitStack
    cm = ExitStack()
    mk = lambda n: Sem(cm.enter_context(nc.semaphore(n)))
    dqw, dqx, dqs, dqo = mk("dqw"), mk("dqx"), mk("dqs"), mk("dqo")
    epe, eact, edve, epool, bar = (mk("epe"), mk("eact"), mk("edve"),
                                   mk("epool"), mk("bar"))
    qx = [mk("qx0"), mk("qx1"), mk("qx2")]
    qw = [mk("qw0"), mk("qw1")]
    qb = [mk("qb0"), mk("qb1")]
    qb2 = [mk("qb20"), mk("qb21")]
    qc = mk("qc")
    qv = mk("qv")
    qst = [mk("qst0"), mk("qst1"), mk("qst2"), mk("qst3")]
    qo = [mk("qo0"), mk("qo1")]

    psA = cm.enter_context(nc.psum_tensor("psA", [128, 1024], F32)).ap()
    psB = cm.enter_context(nc.psum_tensor("psB", [128, 1024], F32)).ap()
    psC = cm.enter_context(nc.psum_tensor("psC", [128, 1024], F32)).ap()
    psD = cm.enter_context(nc.psum_tensor("psD", [128, 1024], F32)).ap()
    psA_bf = psA.tensor.bitcast(BF16).ap()
    psB_bf = psB.tensor.bitcast(BF16).ap()
    psD_bf = psD.tensor.bitcast(BF16).ap()
    psC_bf = psC.tensor.bitcast(BF16).ap()

    P = B()
    nbar = [0]

    def barrier():
        nbar[0] += 1
        P.barrier(bar, nbar[0])

    def dump(name, src_ap, waits):
        if name in dbg:
            P.op("pool", lambda e, n=name, s=src_ap: e.dma_start(
                dbg[n][:, :], s), waits=waits, inc=dqw, inc_k=16)

    def ln_stats_g(src_ap, cset, sqb, waits_first, out, war=None):
        c = {k: mc[:, cset + i:cset + i + 1] for i, k in enumerate(
            ["sx", "ssq", "c2", "c3", "nvar", "rstd", "nmr"])}
        wf = list(waits_first)
        P.op("dve", lambda e: e.reduce_sum(c["sx"], src_ap, axis=AX.X),
             waits=wf, inc=edve)
        yield
        va = P.op("act", lambda e: e.activation(sqb[:, :], src_ap, AF.Square),
                  waits=wf + [war], inc=eact)
        yield
        P.op("dve", lambda e: e.reduce_sum(c["ssq"], sqb[:, :], axis=AX.X),
             waits=[(eact, va)], inc=edve)
        out["r2"] = edve.n
        yield
        P.op("dve", lambda e: e.tensor_mul(c["c2"], c["sx"], c["sx"]),
             inc=edve)
        v3 = edve.n
        yield
        v4 = P.op("act", lambda e: e.mul(c["c3"], c["c2"], 1.0 / D),
                  waits=[(edve, v3)], inc=eact)
        yield
        v5 = P.op("dve", lambda e: e.scalar_tensor_tensor(
            c["nvar"], c["ssq"], D * EPS, c["c3"],
            op0=ALU.add, op1=ALU.subtract), waits=[(eact, v4)], inc=edve)
        yield
        v6 = P.op("act", lambda e: e.activation(c["rstd"], c["nvar"], AF.Ln,
                                                scale=1.0 / D),
                  waits=[(edve, v5)], inc=eact)
        yield
        v7 = P.op("dve", lambda e: e.tensor_scalar(c["rstd"], c["rstd"], -0.5,
                                                   None, op0=ALU.mult),
                  waits=[(eact, v6)], inc=edve)
        yield
        v8 = P.op("act", lambda e: e.activation(c["rstd"], c["rstd"], AF.Exp),
                  waits=[(edve, v7)], inc=eact)
        yield
        v9 = P.op("dve", lambda e: e.tensor_scalar(c["nmr"], c["rstd"],
                                                   c["sx"], -1.0 / D,
                                                   op0=ALU.mult,
                                                   op1=ALU.mult),
                  waits=[(eact, v8)], inc=edve)
        out["nv"] = v9
        out["cols"] = c

    def ln_stats(src_ap, cset, sqb, waits_first=(), war=None):
        out = {}
        for _ in ln_stats_g(src_ap, cset, sqb, list(waits_first), out, war):
            pass
        return out["nv"], out["cols"]

    def drive(gens, W):
        act = []
        pending = list(gens)
        while act or pending:
            while len(act) < W and pending:
                item = pending[0]
                if isinstance(item, tuple):
                    pred, g = item
                    if not pred():
                        break
                    pending.pop(0)
                    act.append(g)
                else:
                    act.append(pending.pop(0))
            if not act:
                if pending:
                    raise RuntimeError("drive stalled on entry predicate")
                break
            g = act.pop(0)
            try:
                next(g)
                act.append(g)
            except StopIteration:
                pass

    for src, dst in [(bq_d, mc[:, C_BQ:C_BQ + 8]), (bk_d, mc[:, C_BK:C_BK + 8]),
                     (bo_d, mc[:, C_BO:C_BO + 8]), (s3_d, mc[:, C_W3S:C_W3S + 8]),
                     (s1_d, mc[:, C_W1S:C_W1S + 32]), (s2_d, mc[:, C_W2S:C_W2S + 32]),
                     (sh1_d, mc[:, C_H1S:C_H1S + 32]), (sh2_d, mc[:, C_H2S:C_H2S + 32])]:
        P.op("pool", lambda e, s=src, d=dst: e.dma_start(d, s[:, :]),
             inc=qc, inc_k=16)
    P.op("pool", lambda e: e.dma_start(mc[:, C_SL:C_SL + H],
                                       sl_d.broadcast_to([128, H])),
         inc=qc, inc_k=16)
    P.op("pool", lambda e: e.dma_start(mrow[0:1, 0:D], bv_d[:, :]),
         inc=qc, inc_k=16)
    v_konst = qc.n

    P.op("pool", lambda e: e.iota(iotf[:, 0:128], pattern=[[1, 128]], base=0,
                                  channel_multiplier=-1,
                                  allow_small_or_imprecise_dtypes=True),
         inc=epool)
    v_io1 = epool.n
    P.op("pool", lambda e: e.iota(
        iotf[:, 128:256].rearrange("p (a b) -> p a b", a=8),
        pattern=[[0, 8], [128, 16]], base=-(S - 1), channel_multiplier=1,
        allow_small_or_imprecise_dtypes=True), inc=epool)
    v_io2 = epool.n
    P.op("pool", lambda e: e.memset(ones2[:, :], 1.0), inc=epool)
    v_ones2 = epool.n
    P.op("dve", lambda e: e.tensor_single_scalar(
        I128[:, :], iotf[:, 0:128], 0.0, op=ALU.is_equal),
        waits=[(epool, v_io1)], inc=edve)
    v_i128 = edve.n
    v1 = P.op("act", lambda e: e.activation(bvhl[0:1, :], mrow[0:1, 0:D],
                                            AF.Copy),
              waits=[(qc, v_konst)], inc=eact)
    v2_ = P.op("dve", lambda e: e.tensor_copy(brow[0:1, 0:D], bvhl[0:1, :]),
               waits=[(eact, v1)], inc=edve)
    for h in range(H):
        P.op("dve", lambda e, h=h: e.tensor_scalar(
            mc[:, C_BIAS + h * 16:C_BIAS + (h + 1) * 16],
            iotf[:, 128 + (h % 8) * 16:128 + (h % 8 + 1) * 16],
            mc[:, C_SL + h:C_SL + h + 1], None, op0=ALU.mult),
            waits=[(epool, v_io2), (qc, v_konst)], inc=edve)
    v_bias = edve.n
    v4_ = P.op("dve", lambda e: e.tensor_sub(brow[0:1, 1024:1024 + D],
                                             mrow[0:1, 0:D],
                                             brow[0:1, 0:D]),
               waits=[(edve, v2_)], inc=edve)
    v5_ = P.op("act", lambda e: e.activation(rowbf[0:1, :],
                                             brow[0:1, 1024:1024 + D],
                                             AF.Copy),
               waits=[(edve, v4_)], inc=eact)
    v_bvhl_dma = P.op("pool", lambda e: e.dma_start(bvhl[1:2, :],
                                                    rowbf[0:1, :]),
                      waits=[(eact, v5_)], inc=qv, inc_k=16)
    v_bvhl = v1
    for k in range(FCH):
        P.op("pool", lambda e, k=k: e.dma_start(
            wv_sb[:, k, :], wv_d[k]), inc=dqw, inc_k=16)
    vw_last = dqw.n
    for tc in range(NCH):
        P.op("pool", lambda e, tc=tc: e.memset(
            v_tm[:, tc, :].rearrange("p (h d) -> p h d", d=65)[:, :, 64:65],
            1.0), inc=epool)
    v_vones = epool.n

    pr4 = [psA, psB, psC, psD]
    pr_free = {}
    slab_free = {}
    a1_si = [0]
    a1_ri = [0]

    def qk_step(wd, dst, cbias, m, th, gate, regs):
        si = a1_si[0]
        a1_si[0] += 1
        sl = si % 2
        vd = P.op("pool", lambda e, wd=wd, m=m, sl=sl: e.dma_start(
            wstage[:, sl, :], wd[m]),
            waits=[(epe, slab_free.get(si - 2))] if si >= 2 else (),
            inc=qw[sl], inc_k=16)
        r = regs[a1_ri[0] % len(regs)]
        a1_ri[0] += 1
        w = [gate, (qw[sl], vd), pr_free.get(r)]
        vp = None
        for k in range(FCH):
            for nh in range(2):
                vp = P.op("pe", lambda e, sl=sl, k=k, th=th, nh=nh,
                          r=r: e.matmul(
                    pr4[r][:, nh * 512:(nh + 1) * 512],
                    wstage[:, sl, k * 128:(k + 1) * 128],
                    ln1[:, k, th * 1024 + nh * 512:
                        th * 1024 + (nh + 1) * 512],
                    start=(k == 0), stop=(k == FCH - 1)),
                    waits=w if (k == 0 and nh == 0) else (),
                    inc=epe if (k == FCH - 1 and nh == 1) else None)
        va = P.op("act", lambda e, dst=dst, m=m, th=th, r=r,
                  cbias=cbias: e.activation(
            dst[:, m, th * 1024:(th + 1) * 1024], pr4[r][:, :],
            AF.Identity, bias=mc[:, cbias + m:cbias + m + 1],
            scale=1.0),
            waits=[(epe, vp), (qc, v_konst)], inc=eact)
        pr_free[r] = (eact, va)
        slab_free[si] = vp

    def v_step(tc, regs):
        r = regs[a1_ri[0] % len(regs)]
        a1_ri[0] += 1
        w = [(edve, a0_ev[tc]), (dqw, vw_last), (eact, v_bvhl),
             (qv, v_bvhl_dma), (epool, v_ones2), (epool, v_vones),
             pr_free.get(r)]
        for k in range(FCH):
            for nh in range(2):
                P.op("pe", lambda e, tc=tc, k=k, nh=nh, r=r: e.matmul(
                    pr4[r][:, nh * 512:(nh + 1) * 512],
                    ln1[:, k, tc * 128:(tc + 1) * 128],
                    wv_sb[:, k, nh * 512:(nh + 1) * 512],
                    start=(k == 0), stop=False),
                    waits=w if (k == 0 and nh == 0) else ())
        vp = None
        for nh in range(2):
            vp = P.op("pe", lambda e, nh=nh, r=r: e.matmul(
                pr4[r][:, nh * 512:(nh + 1) * 512], ones2[:, :],
                bvhl[:, nh * 512:(nh + 1) * 512], start=False, stop=True),
                inc=epe if nh == 1 else None)
        vv = P.op("dve", lambda e, tc=tc, r=r: e.tensor_copy(
            v_tm[:, tc, :].rearrange("p (h d) -> p h d", d=65)[:, :, 0:64],
            pr4[r].rearrange("p (h d) -> p h d", d=64)),
            waits=[(epe, vp)], inc=edve)
        pr_free[r] = (edve, vv)

    aff_done = {}
    a0_tp = {}
    a0_ev = {}
    pend_nv = {}
    r2a0 = {}
    tps01 = [psA_bf[:, 0:1024], psB_bf[:, 0:1024]]

    def a0_chain(tc):
        w = []
        if tc >= 3:
            w += [(eact, aff_done[tc - 3]), (edve, pend_nv[tc - 3])]
        if tc == 2:
            w += [(edve, v4_), (eact, v5_)]
        vx = P.op("sp", lambda e, tc=tc: e.dma_start(
            xbuf[:, tc % 3, :], x_d[tc * 128:(tc + 1) * 128, :]),
            waits=w, inc=qx[tc % 3], inc_k=16)
        yield
        out = {}
        yield from ln_stats_g(xbuf[:, tc % 3, :], C_ST + 16 * (tc % 4),
                              a0sq[:, tc % 2, :], [(qx[tc % 3], vx)], out,
                              war=((edve, lambda tc=tc: r2a0[tc - 2])
                                   if tc >= 2 else None))
        r2a0[tc] = out["r2"]
        nv, cols = out["nv"], out["cols"]
        pend_nv[tc] = nv
        w = [(edve, nv)]
        if tc >= 3:
            w.append((epe, a0_tp[tc - 3]))
        va = P.op("act", lambda e, tc=tc, cols=cols: e.activation(
            a0tm[:, tc % 3, :], xbuf[:, tc % 3, :], AF.Identity,
            bias=cols["nmr"], scale=cols["rstd"]), waits=w, inc=eact)
        aff_done[tc] = va
        yield
        tp = tps01[tc % 2]
        w = [(eact, va), (edve, v_i128)]
        if tc >= 2:
            w.append((edve, lambda tc=tc: a0_ev[tc - 2]))
        vp = None
        for fc in range(FCH):
            vp = P.op("pe", lambda e, tc=tc, fc=fc, tp=tp: e.transpose(
                tp[:, fc * 128:(fc + 1) * 128],
                a0tm[:, tc % 3, fc * 128:(fc + 1) * 128], I128[:, :]),
                waits=w if fc == 0 else (), inc=epe)
        a0_tp[tc] = vp
        yield
        vv = P.op("dve", lambda e, tc=tc, tp=tp: e.tensor_copy(
            ln1[:, :, tc * 128:(tc + 1) * 128],
            tp.rearrange("p (c t) -> p c t", c=FCH)),
            waits=[(epe, vp)], inc=edve)
        a0_ev[tc] = vv

    def one_shot(fn, *a, **kw):
        def g():
            fn(*a, **kw)
            yield
        return g()

    a1_th0 = ([("v", i) for i in range(NCH)] +
              [("q", m) for m in range(FCH)] +
              [("k", m) for m in range(FCH)])

    def a1_emit(i, regs):
        pn, m = a1_th0[i]
        if pn == "v":
            v_step(m, regs=regs)
        else:
            qk_step(wq_d if pn == "q" else wk_d,
                    q_fm if pn == "q" else k_fm,
                    C_BQ if pn == "q" else C_BK,
                    m, 0, (edve, a0_ev[7]), regs=regs)

    def a1_pred(i):
        pn, m = a1_th0[i]
        need = m if pn == "v" else 7
        return lambda need=need: need in a0_ev

    gen_list = [a0_chain(tc) for tc in range(4)]
    k_v = 0
    for tc in range(4, NCH):
        gen_list.append(a0_chain(tc))
        if k_v <= tc - 4:
            gen_list.append((a1_pred(k_v), one_shot(a1_emit, k_v, (2, 3))))
            k_v += 1
    for i in range(k_v, len(a1_th0)):
        gen_list.append((a1_pred(i), one_shot(a1_emit, i, (2, 3))))
    drive(gen_list, 3)
    v_ln1 = edve.n
    v_a0_act = eact.n
    dump("ln1", ln1.rearrange("p c t -> p (c t)"), [(edve, v_ln1)])

    for pn, m in a1_th0:
        if pn == "v":
            continue
        qk_step(wq_d if pn == "q" else wk_d,
                q_fm if pn == "q" else k_fm,
                C_BQ if pn == "q" else C_BK,
                m, 1, (edve, v_ln1), regs=(0, 1, 2, 3))
    v_A1_dve = edve.n
    v_A1_act = eact.n
    v_A1_pe = epe.n
    dump("q", q_fm.rearrange("p c t -> p (c t)"), [(eact, v_A1_act)])
    dump("k", k_fm.rearrange("p c t -> p (c t)"), [(eact, v_A1_act)])
    dump("v", v_tm.rearrange("p c t -> p (c t)"), [(edve, v_A1_dve)])

    barrier()

    exp_done = {}
    p0_rd = {}
    cx_ev = {}
    ctT = {}
    rc_done = {}
    ts_done = {}
    qz_rdy = {}
    s_last = {}
    NIT = H * 2
    P.op("pool", lambda e: e.memset(q_z[64:128, 0, :], 0.0),
         waits=[(epe, v_A1_pe)], inc=epool)
    P.op("pool", lambda e: e.memset(q_z[0:64, 1, :], 0.0), inc=epool)
    v_qz0 = epool.n

    def stage_q(h, extra=()):
        rb, ch = (h % 2) * 64, h // 2
        w = [(epool, v_qz0), (eact, v_A1_act)] + list(extra)
        if h >= 2:
            w.append((epe, s_last[h - 2]))
        qz_rdy[h] = P.op("dve", lambda e, rb=rb, ch=ch, h=h: e.tensor_copy(
            q_z[rb:rb + 64, h % 2, :], q_fm[rb:rb + 64, ch, :]),
            waits=w, inc=edve)
    stage_q(0)
    stage_q(1)

    def ctxT_step(it, t):
        h, qh = divmod(it, 2)
        par = it % 2
        if 2 <= t:
            u = t - 2
            vv = P.op("dve", lambda e, it=it, u=u, h=h, qh=qh:
                      e.tensor_scalar(
                ctx_tm[:, qh * 8 + u, h * 64:(h + 1) * 64],
                psD[:, (u % 2) * 512:(u % 2) * 512 + 64],
                mc[:, C_ST + 32 + u % 4:C_ST + 33 + u % 4], None,
                op0=ALU.mult),
                waits=[(edve, rc_done[(it, u)])], inc=edve)
            ts_done[(it, u)] = vv
        if 1 <= t <= 8:
            u = t - 1
            vr = P.op("dve", lambda e, u=u: e.reciprocal(
                mc[:, C_ST + 32 + u % 4:C_ST + 33 + u % 4],
                psD[:, (u % 2) * 512 + 64:(u % 2) * 512 + 65]),
                waits=[(epe, ctT[(it, u)])], inc=edve)
            rc_done[(it, u)] = vr
        if t <= 7:
            w = [(edve, cx_ev[it])]
            if t >= 2:
                w.append((edve, ts_done[(it, t - 2)]))
            elif (it - 1, 6 + t) in ts_done:
                w.append((edve, ts_done[(it - 1, 6 + t)]))
            vp = P.op("pe", lambda e, it=it, t=t, par=par: e.matmul(
                psD[:, (t % 2) * 512:(t % 2) * 512 + 65],
                ctxu[0:65, par, t * 128:(t + 1) * 128], I128[0:65, 0:65],
                start=True, stop=True), waits=w, inc=epe)
            ctT[(it, t)] = vp

    def emit_S(it, j):
        h, qh = divmod(it, 2)
        ch = h // 2
        sp = psA if j % 2 == 0 else psB
        w = []
        if (it, j - 2) in exp_done:
            w.append((eact, exp_done[(it, j - 2)]))
        elif (it - 1, NCH - 2 + j) in exp_done:
            w.append((eact, exp_done[(it - 1, NCH - 2 + j)]))
        if it == 0 and j < 2:
            w += [(edve, v_A1_dve), (eact, v_A1_act)]
        if j == 0 and qh == 0:
            w.append((edve, qz_rdy[h]))
        vp = None
        for nh in range(2):
            vp = P.op("pe", lambda e, ch=ch, h=h, j=j, qh=qh, nh=nh,
                      sp=sp: e.matmul(
                sp[:, nh * 512:(nh + 1) * 512],
                k_fm[:, ch, j * 128:(j + 1) * 128],
                q_z[:, h % 2,
                    qh * 1024 + nh * 512:qh * 1024 + (nh + 1) * 512],
                start=True, stop=True),
                waits=w if nh == 0 else (), inc=epe if nh == 1 else None)
        if j == NCH - 1 and qh == 1:
            s_last[h] = vp
        w = [(epe, vp)]
        if (it, j - 2) in p0_rd:
            w.append((epe, p0_rd[(it, j - 2)]))
        elif (it - 1, NCH - 2 + j) in p0_rd:
            w.append((epe, p0_rd[(it - 1, NCH - 2 + j)]))
        if it == 0:
            w.append((edve, v_bias))
        va = P.op("act", lambda e, j=j, sp=sp, h=h: e.activation(
            p0buf[:, j % 2, :], sp[:, :], AF.Exp,
            bias=mc[:, C_BIAS + h * 16 + j:C_BIAS + h * 16 + j + 1],
            scale=0.125), waits=w, inc=eact)
        exp_done[(it, j)] = va

    def emit_ctx(it, j):
        h = it // 2
        w = [(eact, exp_done[(it, j)])]
        if j == 0:
            if it > 0:
                w.append((edve, cx_ev[it - 1]))
            if it == 0:
                w += [(edve, v_A1_dve), (epool, v_vones)]
        mm_ = 65
        vp = None
        for nh in range(2):
            vp = P.op("pe", lambda e, j=j, h=h, nh=nh, mm_=mm_: e.matmul(
                psC[0:mm_, nh * 512:(nh + 1) * 512],
                v_tm[:, j, h * 65:h * 65 + mm_],
                p0buf[:, j % 2, nh * 512:(nh + 1) * 512],
                start=(j == 0), stop=(j == NCH - 1)),
                waits=w if nh == 0 else (), inc=epe if nh == 1 else None)
        p0_rd[(it, j)] = vp
        if j == NCH - 1:
            par = it % 2
            w = [(epe, vp)]
            if it >= 2:
                w.append((epe, ctT[(it - 2, 7)]))
            vv = P.op("dve", lambda e, par=par: e.tensor_copy(
                ctxu[0:65, par, :], psC[0:65, :]), waits=w, inc=edve)
            cx_ev[it] = vv

    pend_ctx = None
    for it in range(NIT):
        for j in range(NCH):
            emit_S(it, j)
            if j == 0 and it % 2 == 1 and it // 2 + 1 < H:
                stage_q(it // 2 + 1)
            if 2 <= j <= 11 and it > 0:
                ctxT_step(it - 1, j - 2)
            if pend_ctx is not None:
                emit_ctx(*pend_ctx)
            pend_ctx = (it, j)
    emit_ctx(*pend_ctx)
    for t in range(10):
        ctxT_step(NIT - 1, t)
    v_sdp_dve = edve.n
    v_sdp_act = eact.n
    v_sdp_pe = epe.n
    dump("ctx", ctx_tm.rearrange("p c t -> p (c t)"), [(edve, v_sdp_dve)])

    barrier()

    tps67 = [psD_bf[:, 0:1024], psD_bf[:, 1024:2048]]
    tp_ev = {}
    for tc in range(NCH):
        tp = tps67[tc % 2]
        w = [(edve, v_sdp_dve)] if tc < 2 else [(edve, tp_ev[tc - 2])]
        vp = None
        for fc in range(FCH):
            vp = P.op("pe", lambda e, tc=tc, fc=fc, tp=tp: e.transpose(
                tp[:, fc * 128:(fc + 1) * 128],
                ctx_tm[:, tc, fc * 128:(fc + 1) * 128], I128[:, :]),
                waits=w if fc == 0 else (), inc=epe)
        vv = P.op("dve", lambda e, tc=tc, tp=tp: e.tensor_copy(
            ctx_fm[:, :, tc * 128:(tc + 1) * 128],
            tp.rearrange("p (c t) -> p c t", c=FCH)),
            waits=[(epe, vp)], inc=edve)
        tp_ev[tc] = vv
    v_ctxfm = edve.n

    pr3 = [psA, psB, psC]
    pr_free = {}
    slab_free = {}
    wo_si = [0]
    wo_ri = [0]
    wo_ev = {}

    def wo_step(m, th, regs=(0, 1, 2)):
        si = wo_si[0]
        wo_si[0] += 1
        sl = si % 2
        wsl = [(epe, slab_free.get(si - 2))] if si >= 2 else \
              [(epe, v_sdp_pe)]
        vd = P.op("pool", lambda e, m=m, sl=sl: e.dma_start(
            wstage[:, sl, :], wo_d[m]), waits=wsl, inc=qw[sl], inc_k=16)
        r = regs[wo_ri[0] % len(regs)]
        wo_ri[0] += 1
        w = [(qw[sl], vd), (edve, tp_ev[7] if th == 0 else v_ctxfm),
             pr_free.get(r)]
        if m == 0 and th == 0:
            w.append((eact, v_sdp_act))
        vp = None
        for k in range(FCH):
            for nh in range(2):
                vp = P.op("pe", lambda e, sl=sl, k=k, th=th, nh=nh,
                          r=r: e.matmul(
                    pr3[r][:, nh * 512:(nh + 1) * 512],
                    wstage[:, sl, k * 128:(k + 1) * 128],
                    ctx_fm[:, k, th * 1024 + nh * 512:
                           th * 1024 + (nh + 1) * 512],
                    start=(k == 0), stop=(k == FCH - 1)),
                    waits=w if (k == 0 and nh == 0) else (),
                    inc=epe if (k == FCH - 1 and nh == 1) else None)
        va = P.op("act", lambda e, m=m, th=th, r=r: e.activation(
            att_fm[:, m, th * 1024:(th + 1) * 1024], pr3[r][:, :],
            AF.Identity, bias=mc[:, C_BO + m:C_BO + m + 1], scale=1.0),
            waits=[(epe, vp)], inc=eact)
        pr_free[r] = (eact, va)
        wo_ev[(m, th)] = va
        slab_free[si] = vp

    for m in range(FCH):
        wo_step(m, 0)
    v_wo_th0 = eact.n

    mm = [psA, psB, psC]
    mm_free = {}
    wb_free = {0: (epe, v_sdp_pe), 1: (epe, v_sdp_pe)}
    out_st = {}
    sseq = [0]
    ri_b = [0]
    last_sA = [None]
    TS = {tt: {"h_wr": {}, "g_wr": {}, "g2_wr": {}, "ffn_wr": {},
               "epi_tp": {}, "outadd": {}, "aff4": {}, "r2e": {}}
          for tt in (0, 1)}

    def kloop(sl, coff, n_k, src_fn, w0, regs=(0, 1, 2)):
        r = regs[ri_b[0] % len(regs)]
        ri_b[0] += 1
        w = list(w0) + [mm_free.get(r)]
        vp = None
        for k in range(n_k):
            for nh in range(2):
                vp = P.op("pe", lambda e, sl=sl, coff=coff, k=k, nh=nh,
                          r=r, src_fn=src_fn: e.matmul(
                    mm[r][:, nh * 512:(nh + 1) * 512],
                    wbuf[:, sl, coff + k * 128:coff + (k + 1) * 128],
                    src_fn(k, nh), start=(k == 0), stop=(k == n_k - 1)),
                    waits=w if (k == 0 and nh == 0) else (),
                    inc=epe if (k == n_k - 1 and nh == 1) else None)
        return r, vp

    ln2s0 = lambda k, nh: ln2[:, k, nh * 512:(nh + 1) * 512]
    ln2s1 = lambda k, nh: ln2[:, k, 1024 + nh * 512:1024 + (nh + 1) * 512]
    hs = lambda k, nh: h_buf[:, k, nh * 512:(nh + 1) * 512]
    gs = lambda k, nh: g_buf[:, k, nh * 512:(nh + 1) * 512]

    def w12_step(tt, m, extra=(), regs=(0, 1, 2)):
        st = TS[tt]
        sl = sseq[0] % 2
        sseq[0] += 1
        vd = P.op("pool", lambda e, m=m, sl=sl: e.dma_start(
            wbuf[:, sl, 0:1024], w1_d[m]),
            waits=[wb_free[sl]], inc=qb[sl], inc_k=16)
        vd2 = P.op("pool", lambda e, m=m, sl=sl: e.dma_start(
            wbuf[:, sl, 1024:2048], w2_d[m]), inc=qb2[sl], inc_k=16)
        wA = [(qb[sl], vd)] + list(extra)
        ra, vpa = kloop(sl, 0, 8, ln2s0 if tt == 0 else ln2s1, wA, regs)
        rb_, vpb = kloop(sl, 1024, 8, ln2s0 if tt == 0 else ln2s1,
                         [(qb2[sl], vd2)], regs)
        wb_free[sl] = (epe, vpb)
        w = [(epe, vpa), (qc, v_konst), last_sA[0]]
        va = P.op("act", lambda e, m=m, ra=ra: e.activation(
            sA[:, :], mm[ra][:, :], AF.Silu,
            scale=mc[:, C_W1S + m:C_W1S + m + 1]), waits=w, inc=eact)
        mm_free[ra] = (eact, va)
        vh = P.op("dve", lambda e, m=m, rb_=rb_: e.scalar_tensor_tensor(
            h_buf[:, m, :], mm[rb_][:, :],
            mc[:, C_W2S + m:C_W2S + m + 1], sA[:, :],
            op0=ALU.mult, op1=ALU.mult),
            waits=[(epe, vpb), (eact, va)], inc=edve)
        mm_free[rb_] = (edve, vh)
        last_sA[0] = (edve, vh)
        st["h_wr"][m] = vh

    def big_step(tt, m, wd, src_fn, dst, scol, func, wr_key, extra=()):
        st = TS[tt]
        sl = sseq[0] % 2
        sseq[0] += 1
        vd = P.op("pool", lambda e, m=m, sl=sl, wd=wd: e.dma_start(
            wbuf[:, sl, :], wd[m]), waits=[wb_free[sl]],
            inc=qb[sl], inc_k=16)
        r, vp = kloop(sl, 0, 32, src_fn, [(qb[sl], vd)] + list(extra))
        wb_free[sl] = (epe, vp)
        va = P.op("act", lambda e, m=m, r=r, dst=dst, scol=scol, func=func:
                  e.activation(
            dst[:, m, :], mm[r][:, :], func,
            scale=mc[:, scol + m:scol + m + 1]),
            waits=[(epe, vp)], inc=eact)
        mm_free[r] = (eact, va)
        st[wr_key][m] = va

    def epi_chain(tt, t):
        st = TS[tt]
        v_ffn = (eact, st["ffn_wr"][7])
        tp_ev4 = st["epi_tp"]
        outadd = st["outadd"]
        aff4 = st["aff4"]
        gc = tt * 8 + t
        tp = tps67[t % 2]
        w = [v_ffn]
        if t >= 2:
            w.append((edve, lambda: tp_ev4[t - 2]))
        elif tt == 1:
            w.append((edve, TS[0]["epi_tp"][6 + t]))
        else:
            w.append((edve, tp_ev3.get(NCH - 2 + t)))
            w.append((edve, tp_ev2[NCH - 1]))
        vp = None
        for fc in range(FCH):
            vp = P.op("pe", lambda e, t=t, fc=fc, tp=tp: e.transpose(
                tp[:, fc * 128:(fc + 1) * 128],
                ffn_fm[:, fc, t * 128:(t + 1) * 128], I128[:, :]),
                waits=w if fc == 0 else (), inc=epe)
        yield
        w2_ = [(epe, vp)]
        if t >= 3:
            w2_.append((eact, lambda: aff4[t - 3][1]))
        elif tt == 1:
            w2_.append(TS[0]["aff4"][{0: 6, 1: 7, 2: 5}[t]])
        vv = P.op("dve", lambda e, t=t, tp=tp: e.tensor_copy(
            ffn_tm[:, t % 3, :], tp[:, 0:1024]), waits=w2_, inc=edve)
        tp_ev4[t] = vv
        yield
        w = [(qst[gc % 4], x1st[gc])]
        if t >= 3:
            w.append((edve, lambda: outadd[t - 3]))
        else:
            w.append((eact, st["g2_wr"][31]))
            w.append((edve, tp_ev2[NCH - 1]))
            w.append((edve, v_ln2))
            if tt == 1:
                w.append((edve, TS[0]["outadd"][{0: 6, 1: 7, 2: 5}[t]]))
        vx = P.op("sp", lambda e, gc=gc, t=t: e.dma_start(
            x1rel[:, t % 3, :], x1_d[gc * 128:(gc + 1) * 128, :]),
            waits=w, inc=qx[t % 3], inc_k=16)
        yield
        out = {}
        yield from ln_stats_g(ffn_tm[:, t % 3, :], C_ST + 16 * (t % 4),
                              sq_s[:, t % 2, :], [(edve, vv)], out,
                              war=((edve, lambda t=t, st=st: st["r2e"][t - 2])
                                   if t >= 2 else None))
        st["r2e"][t] = out["r2"]
        nv, cols = out["nv"], out["cols"]
        w = [(edve, nv)]
        if t >= 2:
            w.append((edve, lambda: outadd[t - 2]))
        elif tt == 1:
            w.append((edve, TS[0]["outadd"][6 + t]))
        elif tt == 0:
            w.append((edve, v_ln2))
        va = P.op("act", lambda e, t=t, cols=cols: e.activation(
            ln4t[:, t % 2, :], ffn_tm[:, t % 3, :], AF.Identity,
            bias=cols["nmr"], scale=cols["rstd"]), waits=w, inc=eact)
        aff4[t] = (eact, va)
        yield
        w = [(eact, va), (qx[t % 3], vx)]
        if gc >= 2:
            w.append((qo[gc % 2], lambda gc=gc: out_st[gc - 2][1]))
        if tt == 0 and t < 2:
            w += v_x1st
        vadd = P.op("dve", lambda e, t=t: e.tensor_add(
            xbuf[:, t % 2, :], x1rel[:, t % 3, :], ln4t[:, t % 2, :]),
            waits=w, inc=edve)
        outadd[t] = vadd
        yield
        vst = P.op("act", lambda e, gc=gc, t=t: e.dma_start(
            out_d[gc * 128:(gc + 1) * 128, :], xbuf[:, t % 2, :]),
            waits=[(edve, vadd)], inc=qo[gc % 2], inc_k=16)
        out_st[gc] = (qo[gc % 2], vst)

    def epi_loop(tt, weave=None):
        gens = [epi_chain(tt, t) for t in range(8)]
        drive(gens, 3)

    x0load = {}
    x1st = {}
    aff2 = {}
    tp_ev2 = {}
    tp_ev3 = {}
    x1add = {}
    ln2nv = {}
    r2l3 = {}
    r2l2 = {}
    tpsC = [psC_bf[:, 0:1024], psC_bf[:, 1024:2048]]
    v_wo_act = [None]

    def a3_chain(c):
        tp = tps67[c % 2]
        w = [(eact, (lambda: v_wo_act[0]) if c >= 8 else v_wo_th0)]
        if c >= 2:
            w.append((edve, lambda c=c: tp_ev2[c - 2]))
        else:
            w.append((edve, tp_ev.get(NCH - 2 + c)))
        vp = None
        for fc in range(FCH):
            vp = P.op("pe", lambda e, c=c, fc=fc, tp=tp: e.transpose(
                tp[:, fc * 128:(fc + 1) * 128],
                att_fm[:, fc, c * 128:(c + 1) * 128], I128[:, :]),
                waits=w if fc == 0 else (), inc=epe)
        yield
        w2_ = [(epe, vp)]
        if c >= 4:
            w2_.append((eact, lambda c=c: aff2[c - 4]))
        vv = P.op("dve", lambda e, c=c, tp=tp: e.tensor_copy(
            a3tm[:, c % 4, :], tp[:, 0:1024]), waits=w2_, inc=edve)
        tp_ev2[c] = vv
        yield
        out = {}
        yield from ln_stats_g(a3tm[:, c % 4, :], C_ST + 16 * (c % 4),
                              a3sq[:, c % 2, :], [(edve, vv)], out,
                              war=((edve, lambda c=c: r2l3[c - 2])
                                   if c >= 2 else None))
        r2l3[c] = out["r2"]
        nv, cols = out["nv"], out["cols"]
        w = []
        if c >= 3:
            w += [(eact, lambda c=c: aff2[c - 3]),
                  (edve, lambda c=c: ln2nv[c - 3]),
                  (qst[(c - 3) % 4], lambda c=c: x1st[c - 3])]
        else:
            w.append((edve, v_ln1))
        vx = P.op("sp", lambda e, c=c: e.dma_start(
            xbuf[:, c % 3, :], x_d[c * 128:(c + 1) * 128, :]),
            waits=w, inc=qx[c % 3], inc_k=16)
        x0load[c] = vx
        yield
        w = [(edve, nv)]
        if c >= 2:
            w.append((edve, lambda c=c: x1add[c - 2]))
        va = P.op("act", lambda e, c=c, cols=cols: e.activation(
            a3ln[:, c % 2, :], a3tm[:, c % 4, :], AF.Identity,
            bias=cols["nmr"], scale=cols["rstd"]), waits=w, inc=eact)
        yield
        vadd = P.op("dve", lambda e, c=c: e.tensor_add(
            xbuf[:, c % 3, :], xbuf[:, c % 3, :], a3ln[:, c % 2, :]),
            waits=[(eact, va), (qx[c % 3], x0load[c])], inc=edve)
        x1add[c] = vadd
        yield
        vst = P.op("act", lambda e, c=c: e.dma_start(
            x1_d[c * 128:(c + 1) * 128, :], xbuf[:, c % 3, :]),
            waits=[(edve, vadd)] +
                  ([(qst[c % 4], lambda c=c: x1st[c - 4])] if c >= 4 else []),
            inc=qst[c % 4], inc_k=16)
        x1st[c] = vst
        yield
        out = {}
        yield from ln_stats_g(xbuf[:, c % 3, :], C_ST + 64 + 16 * (c % 4),
                              a3sq[:, 2 + c % 2, :], [(edve, vadd)], out,
                              war=((edve, lambda c=c: r2l2[c - 2])
                                   if c >= 2 else None))
        r2l2[c] = out["r2"]
        nv2, cols2 = out["nv"], out["cols"]
        ln2nv[c] = nv2
        va2 = P.op("act", lambda e, c=c, cols2=cols2: e.activation(
            a3tm[:, c % 4, :], xbuf[:, c % 3, :], AF.Identity,
            bias=cols2["nmr"], scale=cols2["rstd"]),
            waits=[(edve, nv2)], inc=eact)
        aff2[c] = va2
        yield
        tp = tpsC[c % 2]
        w = [(eact, va2)]
        if c >= 2:
            w.append((edve, lambda c=c: tp_ev3[c - 2]))
        else:
            w.append((edve, v_sdp_dve))
        vp = None
        for fc in range(FCH):
            vp = P.op("pe", lambda e, c=c, fc=fc, tp=tp: e.transpose(
                tp[:, fc * 128:(fc + 1) * 128],
                a3tm[:, c % 4, fc * 128:(fc + 1) * 128], I128[:, :]),
                waits=w if fc == 0 else (), inc=epe)
        yield
        vv3 = P.op("dve", lambda e, c=c, tp=tp: e.tensor_copy(
            ln2[:, :, c * 128:(c + 1) * 128],
            tp.rearrange("p (c t) -> p c t", c=FCH)),
            waits=[(epe, vp)], inc=edve)
        tp_ev3[c] = vv3

    def wo1_shot(m):
        def g():
            wo_step(m, 1, regs=(0, 1))
            if m == FCH - 1:
                v_wo_act[0] = eact.n
                dump("att", att_fm.rearrange("p c t -> p (c t)"),
                     [(eact, v_wo_act[0])])
            yield
        return g()

    gen_list = []
    for c in range(NCH):
        gen_list.append(a3_chain(c))
        if c < FCH:
            gen_list.append(wo1_shot(c))
    drive(gen_list, 3)
    v_ln2 = edve.n
    v_x1st = [(s, s.n) for s in qst]
    mm_free[2] = (edve, v_ln2)
    dump("ln2", ln2.rearrange("p c t -> p (c t)"), [(edve, v_ln2)])

    barrier()

    for m in range(32):
        w12_step(0, m, extra=[(edve, tp_ev3[7])])
    dump("h", h_buf.rearrange("p c t -> p (c t)"), [(edve, TS[0]["h_wr"][31])])
    for m in range(32):
        big_step(0, m, h1_d, hs, g_buf, C_H1S, AF.Gelu, "g_wr",
                 extra=[(edve, TS[0]["h_wr"][31])] if m == 0 else ())
    dump("g", g_buf.rearrange("p c t -> p (c t)"), [(eact, TS[0]["g_wr"][31])])
    for m in range(32):
        big_step(0, m, h2_d, gs, h_buf, C_H2S, AF.Gelu, "g2_wr",
                 extra=[(eact, TS[0]["g_wr"][31]),
                        (edve, TS[0]["h_wr"][31])] if m == 0 else ())
    for m in range(8):
        big_step(0, m, w3_d, hs, ffn_fm, C_W3S, AF.Copy, "ffn_wr",
                 extra=[(eact, TS[0]["g2_wr"][31])] if m == 0 else ())
    dump("ffn", ffn_fm.rearrange("p c t -> p (c t)"),
         [(eact, TS[0]["ffn_wr"][7])])

    epi_loop(0)
    for m in range(32):
        w12_step(1, m, extra=[(edve, v_ln2)])
    for m in range(32):
        ex = []
        if m == 0:
            ex = [(edve, TS[1]["h_wr"][31])]
        elif m == 8:
            ex = [(edve, TS[0]["outadd"].get(7, 0)),
                  (eact, TS[0]["aff4"].get(7, (eact, 0))[1])]
        big_step(1, m, h1_d, hs, g_buf, C_H1S, AF.Gelu, "g_wr", extra=ex)
    for m in range(32):
        big_step(1, m, h2_d, gs, h_buf, C_H2S, AF.Gelu, "g2_wr",
                 extra=[(eact, TS[1]["g_wr"][31]),
                        (edve, TS[1]["h_wr"][31])] if m == 0 else ())
    for m in range(8):
        big_step(1, m, w3_d, hs, ffn_fm, C_W3S, AF.Copy, "ffn_wr",
                 extra=[(eact, TS[1]["g2_wr"][31]),
                        (edve, TS[0]["epi_tp"][7])] if m == 0 else ())
    epi_loop(1)

    P.wait("sp", qo[0], qo[0].n)
    P.wait("sp", qo[1], qo[1].n)

    with nc.Block() as block:
        @block.tensor
        def _(e):
            for f in P.q["pe"]:
                f(e)

        @block.scalar
        def _(e):
            for f in P.q["act"]:
                f(e)

        @block.vector
        def _(e):
            for f in P.q["dve"]:
                f(e)

        @block.gpsimd
        def _(e):
            for f in P.q["pool"]:
                f(e)

        @block.sync
        def _(e):
            for f in P.q["sp"]:
                f(e)

    cm.close()
    return nc


def _prep(inputs):
    g = lambda k: np.asarray(inputs[k], dtype=np.float32)

    def slab(w, mm, kk):
        return np.ascontiguousarray(
            w.reshape(mm, 128, kk, 128).transpose(0, 3, 2, 1).reshape(
                mm, 128, kk * 128))

    col = lambda v, n: np.ascontiguousarray(v.reshape(n, 128).T)
    shared = {
        "wq": slab(g("L0_attn_Wq"), 8, 8), "wk": slab(g("L0_attn_Wk"), 8, 8),
        "wo": slab(g("L0_attn_Wo"), 8, 8),
        "wv": np.ascontiguousarray(
            g("L0_attn_Wv").reshape(1024, 8, 128).transpose(1, 2, 0)),
        "w1": slab(g("L0_ffn_w1_w"), 32, 8), "w2": slab(g("L0_ffn_w2_w"), 32, 8),
        "h1": slab(g("L0_ffn_h1_w"), 32, 32), "h2": slab(g("L0_ffn_h2_w"), 32, 32),
        "w3": slab(g("L0_ffn_w3_w"), 8, 32),
        "bq": col(g("L0_attn_bq"), 8), "bk": col(g("L0_attn_bk"), 8),
        "bo": col(g("L0_attn_bo"), 8),
        "bv": np.ascontiguousarray(g("L0_attn_bv").reshape(1, 1024)),
        "sl": np.ascontiguousarray(g("L0_attn_slopes").reshape(1, 16)),
        "s1": col(g("L0_ffn_w1_s"), 32), "s2": col(g("L0_ffn_w2_s"), 32),
        "sh1": col(g("L0_ffn_h1_s"), 32), "sh2": col(g("L0_ffn_h2_s"), 32),
        "s3": col(g("L0_ffn_w3_s"), 8),
    }
    x = np.ascontiguousarray(np.asarray(inputs["L0_attn_x"], dtype=np.float32))
    return [dict(shared, x=np.ascontiguousarray(x[c])) for c in range(8)]


def kernel(**inputs):
    maps = _prep(inputs)
    nc = build_nc()
    res = run_bass_kernel_spmd(nc, maps, core_ids=list(range(8)))
    return np.stack([np.asarray(r["out"]) for r in res.results],
                    axis=0).astype(np.float32)


# version 17 (final)
# speedup vs baseline: 1.0064x; 1.0064x over previous
import numpy as np
import concourse.bass as bass
import concourse.mybir as mybir
from concourse.bass_utils import run_bass_kernel_spmd

F32 = mybir.dt.float32
BF16 = mybir.dt.bfloat16
AF = mybir.ActivationFunctionType
ALU = mybir.AluOpType
AX = mybir.AxisListType

S = 2048
D = 1024
H = 16
FF = 4096
NCH = S // 128
FCH = D // 128
EPS = 1e-5

ENGS = ("pe", "act", "dve", "pool", "sp")


class Sem:
    def __init__(self, h):
        self.h = h
        self.n = 0


class B:

    def __init__(self):
        self.q = {e: [] for e in ENGS}
        self.hw = {e: {} for e in ENGS}
        self.meta = {e: [] for e in ENGS}

    def wait(self, eng, sem, val):
        if val is None:
            return
        if callable(val):
            self.q[eng].append(lambda e, s=sem.h, v=val: e.wait_ge(s, v()))
            self.meta[eng].append(((sem, val), None, 0))
            return
        if val <= 0:
            return
        if self.hw[eng].get(id(sem), -1) >= val:
            return
        self.hw[eng][id(sem)] = val
        self.q[eng].append(lambda e, s=sem.h, v=val: e.wait_ge(s, v))
        self.meta[eng].append(((sem, val), None, 0))

    def op(self, eng, make, waits=(), inc=None, inc_k=1):
        for w in waits:
            if w is None:
                continue
            sem, val = w
            self.wait(eng, sem, val)
        if inc is None:
            self.q[eng].append(make)
            self.meta[eng].append((None, None, 0))
            return None
        inc.n += inc_k
        self.q[eng].append(
            lambda e, m=make, s=inc.h, k=inc_k: m(e).then_inc(s, k))
        self.meta[eng].append((None, inc, inc_k))
        return inc.n

    def barrier(self, bar, idx):
        for e in ENGS:
            bar.n += 1
            self.q[e].append(lambda en, s=bar.h: en.sem_inc(s, 1))
            self.meta[e].append((None, bar, 1))
        for e in ENGS:
            self.wait(e, bar, 5 * idx)

    def check_deadlock(self):
        pos = {e: 0 for e in ENGS}
        cnt = {}
        while True:
            progress = False
            for e in ENGS:
                while pos[e] < len(self.meta[e]):
                    w, inc, k = self.meta[e][pos[e]]
                    if w is not None:
                        sem, val = w
                        if callable(val):
                            val = val()
                        if val is not None and val > 0 and \
                                cnt.get(id(sem), 0) < val:
                            break
                    if inc is not None:
                        cnt[id(inc)] = cnt.get(id(inc), 0) + k
                    pos[e] += 1
                    progress = True
            if not progress:
                break
        stuck = {e: (pos[e], len(self.meta[e])) for e in ENGS
                 if pos[e] < len(self.meta[e])}
        return stuck


def build_nc(debug=()):
    nc = bass.Bass("TRN2", target_bir_lowering=False, debug=False)
    debug = set(debug)

    din = lambda n, s: nc.dram_tensor(n, s, F32, kind="ExternalInput").ap()
    x_d = din("x", [S, D])
    wq_d = din("wq", [8, 128, 1024])
    wk_d = din("wk", [8, 128, 1024])
    wo_d = din("wo", [8, 128, 1024])
    wv_d = din("wv", [8, 128, 1024])
    w1_d = din("w1", [32, 128, 1024])
    w2_d = din("w2", [32, 128, 1024])
    h1_d = din("h1", [32, 128, 4096])
    h2_d = din("h2", [32, 128, 4096])
    w3_d = din("w3", [8, 128, 4096])
    bq_d = din("bq", [128, 8])
    bk_d = din("bk", [128, 8])
    bo_d = din("bo", [128, 8])
    bv_d = din("bv", [1, D])
    sl_d = din("sl", [1, H])
    s1_d = din("s1", [128, 32])
    s2_d = din("s2", [128, 32])
    sh1_d = din("sh1", [128, 32])
    sh2_d = din("sh2", [128, 32])
    s3_d = din("s3", [128, 8])
    out_d = nc.dram_tensor("out", [S, D], F32, kind="ExternalOutput").ap()
    x1_d = nc.dram_tensor("x1d", [S, D], F32, kind="Internal").ap()
    dbg = {}
    for name in debug:
        shp = {"ln1": [128, FCH * S], "q": [128, FCH * S], "k": [128, FCH * S],
               "v": [128, NCH * H * 65], "ctx": [128, NCH * D],
               "att": [128, FCH * S], "ln2": [128, FCH * S],
               "h": [128, 32 * 1024], "g": [128, 32 * 1024],
               "ffn": [128, 8 * 1024], "etm0": [128, 1024],
               "x1r0": [128, 1024]}[name]
        dbg[name] = nc.dram_tensor("dbg_" + name, shp, F32,
                                   kind="ExternalOutput").ap()

    SB0 = (nc.sbuf_base + 31) // 32 * 32
    def at(name, off, shape, dtype):
        return nc.alloc_sbuf_tensor_at(name, list(shape), dtype,
                                       offset=SB0 + off).ap()

    ln1 = at("ln1", 0, [128, FCH, S], BF16)
    q_fm = at("q_fm", 32768, [128, FCH, S], BF16)
    k_fm = at("k_fm", 65536, [128, FCH, S], BF16)
    v_tm = at("v_tm", 98304, [128, NCH, H * 65], BF16)
    ctx_tm = at("ctx_tm", 131584, [128, NCH, D], BF16)
    ln2 = at("ln2", 131584, [128, FCH, S], BF16)
    h_buf = at("h_buf", 0, [128, 32, 1024], BF16)
    g_buf = at("g_buf", 65536, [128, 32, 1024], BF16)
    ctx_fm = at("ctx_fm", 32768, [128, FCH, S], BF16)
    att_fm = at("att_fm", 65536, [128, FCH, S], BF16)
    ffn_fm = at("ffn_fm", 65536, [128, 8, 1024], BF16)
    ffn_tm = at("ffn_tm", 81920, [128, 3, 1024], BF16)
    x1rel = at("x1rel", 88064, [128, 3, 1024], F32)
    sq_s = at("sq_s", 100352, [128, 2, 1024], F32)
    ln4t = at("ln4t", 108544, [128, 2, 1024], F32)
    a3tm = at("a3tm", 98304, [128, 4, 1024], BF16)
    a3sq = at("a3sq", 106496, [128, 4, 1024], F32)
    a3ln = at("a3ln", 122880, [128, 2, 1024], F32)
    wv_sb = at("wv_sb", 164352, [128, FCH, D], BF16)
    p0buf = at("p0buf", 164352, [128, 2, 1024], BF16)
    ctxu = at("ctxu", 168448, [128, 2, 1024], BF16)
    q_z = at("q_z", 172544, [128, 2, 2048], BF16)
    wbuf = at("wbuf", 164352, [128, 2, 4096], BF16)
    wstage = at("wstage", 180736, [128, 2, 1024], BF16)
    sA = at("sA", 180736, [128, 1024], F32)
    rowbf = at("rowbf", 182784, [1, 1024], BF16)
    xbuf = at("xbuf", 184832, [128, 3, 1024], F32)
    mrow = at("mrow", 193024, [128, 1024], F32)
    mc = at("mc", 197120, [128, 640], F32)
    bvhl = at("bvhl", 199680, [2, 1024], BF16)
    ones2 = at("ones2", 201728, [2, 128], BF16)
    I128 = at("I128", 202016, [128, 128], BF16)
    iotf = at("iotf", 202272, [128, 256], F32)
    brow = at("brow", 203296, [1, 2048], F32)
    a0tm = at("a0tm", 131584, [128, 3, 1024], BF16)
    a0sq = at("a0sq", 137728, [128, 2, 1024], F32)
    C_BQ, C_BK, C_BO, C_W3S = 0, 8, 16, 24
    C_W1S, C_W2S, C_H1S, C_H2S = 32, 64, 96, 128
    C_SL, C_BIAS, C_ST = 160, 176, 432

    from contextlib import ExitStack
    cm = ExitStack()
    mk = lambda n: Sem(cm.enter_context(nc.semaphore(n)))
    dqw, dqx, dqs, dqo = mk("dqw"), mk("dqx"), mk("dqs"), mk("dqo")
    epe, eact, edve, epool, bar = (mk("epe"), mk("eact"), mk("edve"),
                                   mk("epool"), mk("bar"))
    qx = [mk("qx0"), mk("qx1"), mk("qx2")]
    qw = [mk("qw0"), mk("qw1")]
    qb = [mk("qb0"), mk("qb1")]
    qb2 = [mk("qb20"), mk("qb21")]
    qc = mk("qc")
    qv = mk("qv")
    qst = [mk("qst0"), mk("qst1"), mk("qst2"), mk("qst3")]
    qo = [mk("qo0"), mk("qo1")]

    psA = cm.enter_context(nc.psum_tensor("psA", [128, 1024], F32)).ap()
    psB = cm.enter_context(nc.psum_tensor("psB", [128, 1024], F32)).ap()
    psC = cm.enter_context(nc.psum_tensor("psC", [128, 1024], F32)).ap()
    psD = cm.enter_context(nc.psum_tensor("psD", [128, 1024], F32)).ap()
    psA_bf = psA.tensor.bitcast(BF16).ap()
    psB_bf = psB.tensor.bitcast(BF16).ap()
    psD_bf = psD.tensor.bitcast(BF16).ap()
    psC_bf = psC.tensor.bitcast(BF16).ap()

    P = B()
    nbar = [0]

    def barrier():
        nbar[0] += 1
        P.barrier(bar, nbar[0])

    def dump(name, src_ap, waits):
        if name in dbg:
            P.op("pool", lambda e, n=name, s=src_ap: e.dma_start(
                dbg[n][:, :], s), waits=waits, inc=dqw, inc_k=16)

    def ln_stats_g(src_ap, cset, sqb, waits_first, out, war=None):
        c = {k: mc[:, cset + i:cset + i + 1] for i, k in enumerate(
            ["sx", "ssq", "c2", "c3", "nvar", "rstd", "nmr"])}
        wf = list(waits_first)
        v1s = P.op("dve", lambda e: e.reduce_sum(c["sx"], src_ap, axis=AX.X),
                   waits=wf, inc=edve)
        yield
        va = P.op("act", lambda e: e.activation(sqb[:, :], src_ap, AF.Square),
                  waits=wf + [war], inc=eact)
        yield
        P.op("dve", lambda e: e.reduce_sum(c["ssq"], sqb[:, :], axis=AX.X),
             waits=[(eact, va)], inc=edve)
        out["r2"] = edve.n
        r2v = out["r2"]
        yield
        P.op("dve", lambda e: e.tensor_mul(c["c2"], c["sx"], c["sx"]),
             waits=[(edve, v1s)], inc=edve)
        v3 = edve.n
        yield
        v4 = P.op("act", lambda e: e.mul(c["c3"], c["c2"], 1.0 / D),
                  waits=[(edve, v3)], inc=eact)
        yield
        v5 = P.op("dve", lambda e: e.scalar_tensor_tensor(
            c["nvar"], c["ssq"], D * EPS, c["c3"],
            op0=ALU.add, op1=ALU.subtract),
            waits=[(eact, v4), (edve, r2v)], inc=edve)
        yield
        v6 = P.op("act", lambda e: e.activation(c["rstd"], c["nvar"], AF.Ln,
                                                scale=1.0 / D),
                  waits=[(edve, v5)], inc=eact)
        yield
        v7 = P.op("dve", lambda e: e.tensor_scalar(c["rstd"], c["rstd"], -0.5,
                                                   None, op0=ALU.mult),
                  waits=[(eact, v6)], inc=edve)
        yield
        v8 = P.op("act", lambda e: e.activation(c["rstd"], c["rstd"], AF.Exp),
                  waits=[(edve, v7)], inc=eact)
        yield
        v9 = P.op("dve", lambda e: e.tensor_scalar(c["nmr"], c["rstd"],
                                                   c["sx"], -1.0 / D,
                                                   op0=ALU.mult,
                                                   op1=ALU.mult),
                  waits=[(eact, v8), (edve, v1s)], inc=edve)
        out["nv"] = v9
        out["cols"] = c

    def ln_stats(src_ap, cset, sqb, waits_first=(), war=None):
        out = {}
        for _ in ln_stats_g(src_ap, cset, sqb, list(waits_first), out, war):
            pass
        return out["nv"], out["cols"]

    def drive(gens, W):
        act = []
        pending = list(gens)
        while act or pending:
            while len(act) < W and pending:
                item = pending[0]
                if isinstance(item, tuple):
                    pred, g = item
                    if not pred():
                        break
                    pending.pop(0)
                    act.append(g)
                else:
                    act.append(pending.pop(0))
            if not act:
                if pending:
                    raise RuntimeError("drive stalled on entry predicate")
                break
            g = act.pop(0)
            try:
                next(g)
                act.append(g)
            except StopIteration:
                pass

    for src, dst in [(bq_d, mc[:, C_BQ:C_BQ + 8]), (bk_d, mc[:, C_BK:C_BK + 8]),
                     (bo_d, mc[:, C_BO:C_BO + 8]), (s3_d, mc[:, C_W3S:C_W3S + 8]),
                     (s1_d, mc[:, C_W1S:C_W1S + 32]), (s2_d, mc[:, C_W2S:C_W2S + 32]),
                     (sh1_d, mc[:, C_H1S:C_H1S + 32]), (sh2_d, mc[:, C_H2S:C_H2S + 32])]:
        P.op("pool", lambda e, s=src, d=dst: e.dma_start(d, s[:, :]),
             inc=qc, inc_k=16)
    P.op("pool", lambda e: e.dma_start(mc[:, C_SL:C_SL + H],
                                       sl_d.broadcast_to([128, H])),
         inc=qc, inc_k=16)
    P.op("pool", lambda e: e.dma_start(mrow[0:1, 0:D], bv_d[:, :]),
         inc=qc, inc_k=16)
    v_konst = qc.n

    P.op("pool", lambda e: e.iota(iotf[:, 0:128], pattern=[[1, 128]], base=0,
                                  channel_multiplier=-1,
                                  allow_small_or_imprecise_dtypes=True),
         inc=epool)
    v_io1 = epool.n
    P.op("pool", lambda e: e.iota(
        iotf[:, 128:256].rearrange("p (a b) -> p a b", a=8),
        pattern=[[0, 8], [128, 16]], base=-(S - 1), channel_multiplier=1,
        allow_small_or_imprecise_dtypes=True), inc=epool)
    v_io2 = epool.n
    P.op("pool", lambda e: e.memset(ones2[:, :], 1.0), inc=epool)
    v_ones2 = epool.n
    P.op("dve", lambda e: e.tensor_single_scalar(
        I128[:, :], iotf[:, 0:128], 0.0, op=ALU.is_equal),
        waits=[(epool, v_io1)], inc=edve)
    v_i128 = edve.n
    v1 = P.op("act", lambda e: e.activation(bvhl[0:1, :], mrow[0:1, 0:D],
                                            AF.Copy),
              waits=[(qc, v_konst)], inc=eact)
    v2_ = P.op("dve", lambda e: e.tensor_copy(brow[0:1, 0:D], bvhl[0:1, :]),
               waits=[(eact, v1)], inc=edve)
    for h in range(H):
        P.op("dve", lambda e, h=h: e.tensor_scalar(
            mc[:, C_BIAS + h * 16:C_BIAS + (h + 1) * 16],
            iotf[:, 128 + (h % 8) * 16:128 + (h % 8 + 1) * 16],
            mc[:, C_SL + h:C_SL + h + 1], None, op0=ALU.mult),
            waits=[(epool, v_io2), (qc, v_konst)], inc=edve)
    v_bias = edve.n
    v4_ = P.op("dve", lambda e: e.tensor_sub(brow[0:1, 1024:1024 + D],
                                             mrow[0:1, 0:D],
                                             brow[0:1, 0:D]),
               waits=[(edve, v2_)], inc=edve)
    v5_ = P.op("act", lambda e: e.activation(rowbf[0:1, :],
                                             brow[0:1, 1024:1024 + D],
                                             AF.Copy),
               waits=[(edve, v4_)], inc=eact)
    v_bvhl_dma = P.op("pool", lambda e: e.dma_start(bvhl[1:2, :],
                                                    rowbf[0:1, :]),
                      waits=[(eact, v5_)], inc=qv, inc_k=16)
    v_bvhl = v1
    for k in range(FCH):
        P.op("pool", lambda e, k=k: e.dma_start(
            wv_sb[:, k, :], wv_d[k]), inc=dqw, inc_k=16)
    vw_last = dqw.n
    for tc in range(NCH):
        P.op("pool", lambda e, tc=tc: e.memset(
            v_tm[:, tc, :].rearrange("p (h d) -> p h d", d=65)[:, :, 64:65],
            1.0), inc=epool)
    v_vones = epool.n

    pr4 = [psA, psB, psC, psD]
    pr_free = {}
    slab_free = {}
    a1_si = [0]
    a1_ri = [0]

    def qk_step(wd, dst, cbias, m, th, gate, regs):
        si = a1_si[0]
        a1_si[0] += 1
        sl = si % 2
        vd = P.op("pool", lambda e, wd=wd, m=m, sl=sl: e.dma_start(
            wstage[:, sl, :], wd[m]),
            waits=([(epe, slab_free.get(si - 2))] if si >= 2 else
                   [(qv, v_bvhl_dma)] if si == 1 else ()),
            inc=qw[sl], inc_k=16)
        r = regs[a1_ri[0] % len(regs)]
        a1_ri[0] += 1
        w = [gate, (qw[sl], vd), pr_free.get(r)]
        vp = None
        for k in range(FCH):
            for nh in range(2):
                vp = P.op("pe", lambda e, sl=sl, k=k, th=th, nh=nh,
                          r=r: e.matmul(
                    pr4[r][:, nh * 512:(nh + 1) * 512],
                    wstage[:, sl, k * 128:(k + 1) * 128],
                    ln1[:, k, th * 1024 + nh * 512:
                        th * 1024 + (nh + 1) * 512],
                    start=(k == 0), stop=(k == FCH - 1)),
                    waits=w if (k == 0 and nh == 0) else (),
                    inc=epe if (k == FCH - 1 and nh == 1) else None)
        va = P.op("act", lambda e, dst=dst, m=m, th=th, r=r,
                  cbias=cbias: e.activation(
            dst[:, m, th * 1024:(th + 1) * 1024], pr4[r][:, :],
            AF.Identity, bias=mc[:, cbias + m:cbias + m + 1],
            scale=1.0),
            waits=[(epe, vp), (qc, v_konst)], inc=eact)
        pr_free[r] = (eact, va)
        slab_free[si] = vp

    def v_step(tc, regs):
        r = regs[a1_ri[0] % len(regs)]
        a1_ri[0] += 1
        w = [(edve, a0_ev[tc]), (dqw, vw_last), (eact, v_bvhl),
             (qv, v_bvhl_dma), (epool, v_ones2), (epool, v_vones),
             pr_free.get(r)]
        for k in range(FCH):
            for nh in range(2):
                P.op("pe", lambda e, tc=tc, k=k, nh=nh, r=r: e.matmul(
                    pr4[r][:, nh * 512:(nh + 1) * 512],
                    ln1[:, k, tc * 128:(tc + 1) * 128],
                    wv_sb[:, k, nh * 512:(nh + 1) * 512],
                    start=(k == 0), stop=False),
                    waits=w if (k == 0 and nh == 0) else ())
        vp = None
        for nh in range(2):
            vp = P.op("pe", lambda e, nh=nh, r=r: e.matmul(
                pr4[r][:, nh * 512:(nh + 1) * 512], ones2[:, :],
                bvhl[:, nh * 512:(nh + 1) * 512], start=False, stop=True),
                inc=epe if nh == 1 else None)
        vv = P.op("dve", lambda e, tc=tc, r=r: e.tensor_copy(
            v_tm[:, tc, :].rearrange("p (h d) -> p h d", d=65)[:, :, 0:64],
            pr4[r].rearrange("p (h d) -> p h d", d=64)),
            waits=[(epe, vp)], inc=edve)
        pr_free[r] = (edve, vv)

    aff_done = {}
    a0_tp = {}
    a0_ev = {}
    pend_nv = {}
    r2a0 = {}
    tps01 = [psA_bf[:, 0:1024], psB_bf[:, 0:1024]]

    def a0_chain(tc):
        w = []
        if tc >= 3:
            w += [(eact, aff_done[tc - 3]), (edve, pend_nv[tc - 3])]
        if tc == 2:
            w += [(edve, v4_), (eact, v5_)]
        vx = P.op("sp", lambda e, tc=tc: e.dma_start(
            xbuf[:, tc % 3, :], x_d[tc * 128:(tc + 1) * 128, :]),
            waits=w, inc=qx[tc % 3], inc_k=16)
        yield
        out = {}
        yield from ln_stats_g(xbuf[:, tc % 3, :], C_ST + 16 * (tc % 4),
                              a0sq[:, tc % 2, :], [(qx[tc % 3], vx)], out,
                              war=((edve, lambda tc=tc: r2a0[tc - 2])
                                   if tc >= 2 else None))
        r2a0[tc] = out["r2"]
        nv, cols = out["nv"], out["cols"]
        pend_nv[tc] = nv
        w = [(edve, nv)]
        if tc >= 3:
            w.append((epe, a0_tp[tc - 3]))
        va = P.op("act", lambda e, tc=tc, cols=cols: e.activation(
            a0tm[:, tc % 3, :], xbuf[:, tc % 3, :], AF.Identity,
            bias=cols["nmr"], scale=cols["rstd"]), waits=w, inc=eact)
        aff_done[tc] = va
        yield
        tp = tps01[tc % 2]
        w = [(eact, va), (edve, v_i128)]
        if tc >= 2:
            w.append((edve, lambda tc=tc: a0_ev[tc - 2]))
        vp = None
        for fc in range(FCH):
            vp = P.op("pe", lambda e, tc=tc, fc=fc, tp=tp: e.transpose(
                tp[:, fc * 128:(fc + 1) * 128],
                a0tm[:, tc % 3, fc * 128:(fc + 1) * 128], I128[:, :]),
                waits=w if fc == 0 else (), inc=epe)
        a0_tp[tc] = vp
        yield
        vv = P.op("dve", lambda e, tc=tc, tp=tp: e.tensor_copy(
            ln1[:, :, tc * 128:(tc + 1) * 128],
            tp.rearrange("p (c t) -> p c t", c=FCH)),
            waits=[(epe, vp)], inc=edve)
        a0_ev[tc] = vv

    def one_shot(fn, *a, **kw):
        def g():
            fn(*a, **kw)
            yield
        return g()

    a1_th0 = ([("v", i) for i in range(NCH)] +
              [("q", m) for m in range(FCH)] +
              [("k", m) for m in range(FCH)])

    def a1_emit(i, regs):
        pn, m = a1_th0[i]
        if pn == "v":
            v_step(m, regs=regs)
        else:
            qk_step(wq_d if pn == "q" else wk_d,
                    q_fm if pn == "q" else k_fm,
                    C_BQ if pn == "q" else C_BK,
                    m, 0, (edve, a0_ev[7]), regs=regs)

    def a1_pred(i):
        pn, m = a1_th0[i]
        need = m if pn == "v" else 7
        return lambda need=need: need in a0_ev

    gen_list = [a0_chain(tc) for tc in range(4)]
    k_v = 0
    for tc in range(4, NCH):
        gen_list.append(a0_chain(tc))
        if k_v <= tc - 4:
            gen_list.append((a1_pred(k_v), one_shot(a1_emit, k_v, (2, 3))))
            k_v += 1
    for i in range(k_v, len(a1_th0)):
        gen_list.append((a1_pred(i), one_shot(a1_emit, i, (2, 3))))
    drive(gen_list, 3)
    v_ln1 = edve.n
    v_a0_act = eact.n
    dump("ln1", ln1.rearrange("p c t -> p (c t)"), [(edve, v_ln1)])

    for pn, m in a1_th0:
        if pn == "v":
            continue
        qk_step(wq_d if pn == "q" else wk_d,
                q_fm if pn == "q" else k_fm,
                C_BQ if pn == "q" else C_BK,
                m, 1, (edve, v_ln1), regs=(0, 1, 2, 3))
    v_A1_dve = edve.n
    v_A1_act = eact.n
    v_A1_pe = epe.n
    dump("q", q_fm.rearrange("p c t -> p (c t)"), [(eact, v_A1_act)])
    dump("k", k_fm.rearrange("p c t -> p (c t)"), [(eact, v_A1_act)])
    dump("v", v_tm.rearrange("p c t -> p (c t)"), [(edve, v_A1_dve)])

    barrier()

    exp_done = {}
    p0_rd = {}
    cx_ev = {}
    ctT = {}
    rc_done = {}
    ts_done = {}
    qz_rdy = {}
    s_last = {}
    NIT = H * 2
    P.op("pool", lambda e: e.memset(q_z[64:128, 0, :], 0.0),
         waits=[(epe, v_A1_pe)], inc=epool)
    P.op("pool", lambda e: e.memset(q_z[0:64, 1, :], 0.0), inc=epool)
    v_qz0 = epool.n

    def stage_q(h, extra=()):
        rb, ch = (h % 2) * 64, h // 2
        w = [(epool, v_qz0), (eact, v_A1_act)] + list(extra)
        if h >= 2:
            w.append((epe, s_last[h - 2]))
        qz_rdy[h] = P.op("dve", lambda e, rb=rb, ch=ch, h=h: e.tensor_copy(
            q_z[rb:rb + 64, h % 2, :], q_fm[rb:rb + 64, ch, :]),
            waits=w, inc=edve)
    stage_q(0)
    stage_q(1)

    def ctxT_step(it, t):
        h, qh = divmod(it, 2)
        par = it % 2
        if 2 <= t:
            u = t - 2
            vv = P.op("dve", lambda e, it=it, u=u, h=h, qh=qh:
                      e.tensor_scalar(
                ctx_tm[:, qh * 8 + u, h * 64:(h + 1) * 64],
                psD[:, (u % 2) * 512:(u % 2) * 512 + 64],
                mc[:, C_ST + 32 + u % 4:C_ST + 33 + u % 4], None,
                op0=ALU.mult),
                waits=[(edve, rc_done[(it, u)])], inc=edve)
            ts_done[(it, u)] = vv
        if 1 <= t <= 8:
            u = t - 1
            vr = P.op("dve", lambda e, u=u: e.reciprocal(
                mc[:, C_ST + 32 + u % 4:C_ST + 33 + u % 4],
                psD[:, (u % 2) * 512 + 64:(u % 2) * 512 + 65]),
                waits=[(epe, ctT[(it, u)])], inc=edve)
            rc_done[(it, u)] = vr
        if t <= 7:
            w = [(edve, cx_ev[it])]
            if t >= 2:
                w.append((edve, ts_done[(it, t - 2)]))
            elif (it - 1, 6 + t) in ts_done:
                w.append((edve, ts_done[(it - 1, 6 + t)]))
            vp = P.op("pe", lambda e, it=it, t=t, par=par: e.matmul(
                psD[:, (t % 2) * 512:(t % 2) * 512 + 65],
                ctxu[0:65, par, t * 128:(t + 1) * 128], I128[0:65, 0:65],
                start=True, stop=True), waits=w, inc=epe)
            ctT[(it, t)] = vp

    def emit_S(it, j):
        h, qh = divmod(it, 2)
        ch = h // 2
        sp = psA if j % 2 == 0 else psB
        w = []
        if (it, j - 2) in exp_done:
            w.append((eact, exp_done[(it, j - 2)]))
        elif (it - 1, NCH - 2 + j) in exp_done:
            w.append((eact, exp_done[(it - 1, NCH - 2 + j)]))
        if it == 0 and j < 2:
            w += [(edve, v_A1_dve), (eact, v_A1_act)]
        if j == 0 and qh == 0:
            w.append((edve, qz_rdy[h]))
        vp = None
        for nh in range(2):
            vp = P.op("pe", lambda e, ch=ch, h=h, j=j, qh=qh, nh=nh,
                      sp=sp: e.matmul(
                sp[:, nh * 512:(nh + 1) * 512],
                k_fm[:, ch, j * 128:(j + 1) * 128],
                q_z[:, h % 2,
                    qh * 1024 + nh * 512:qh * 1024 + (nh + 1) * 512],
                start=True, stop=True),
                waits=w if nh == 0 else (), inc=epe if nh == 1 else None)
        if j == NCH - 1 and qh == 1:
            s_last[h] = vp
        w = [(epe, vp)]
        if (it, j - 2) in p0_rd:
            w.append((epe, p0_rd[(it, j - 2)]))
        elif (it - 1, NCH - 2 + j) in p0_rd:
            w.append((epe, p0_rd[(it - 1, NCH - 2 + j)]))
        if it == 0:
            w.append((edve, v_bias))
        va = P.op("act", lambda e, j=j, sp=sp, h=h: e.activation(
            p0buf[:, j % 2, :], sp[:, :], AF.Exp,
            bias=mc[:, C_BIAS + h * 16 + j:C_BIAS + h * 16 + j + 1],
            scale=0.125), waits=w, inc=eact)
        exp_done[(it, j)] = va

    def emit_ctx(it, j):
        h = it // 2
        w = [(eact, exp_done[(it, j)])]
        if j == 0:
            if it > 0:
                w.append((edve, cx_ev[it - 1]))
            if it == 0:
                w += [(edve, v_A1_dve), (epool, v_vones)]
        mm_ = 65
        vp = None
        for nh in range(2):
            vp = P.op("pe", lambda e, j=j, h=h, nh=nh, mm_=mm_: e.matmul(
                psC[0:mm_, nh * 512:(nh + 1) * 512],
                v_tm[:, j, h * 65:h * 65 + mm_],
                p0buf[:, j % 2, nh * 512:(nh + 1) * 512],
                start=(j == 0), stop=(j == NCH - 1)),
                waits=w if nh == 0 else (), inc=epe if nh == 1 else None)
        p0_rd[(it, j)] = vp
        if j == NCH - 1:
            par = it % 2
            w = [(epe, vp)]
            if it >= 2:
                w.append((epe, ctT[(it - 2, 7)]))
            vv = P.op("dve", lambda e, par=par: e.tensor_copy(
                ctxu[0:65, par, :], psC[0:65, :]), waits=w, inc=edve)
            cx_ev[it] = vv

    pend_ctx = None
    for it in range(NIT):
        for j in range(NCH):
            emit_S(it, j)
            if j == 0 and it % 2 == 1 and 2 <= it // 2 + 1 < H:
                stage_q(it // 2 + 1)
            if 2 <= j <= 11 and it > 0:
                ctxT_step(it - 1, j - 2)
            if pend_ctx is not None:
                emit_ctx(*pend_ctx)
            pend_ctx = (it, j)
    emit_ctx(*pend_ctx)
    for t in range(10):
        ctxT_step(NIT - 1, t)
    v_sdp_dve = edve.n
    v_sdp_act = eact.n
    v_sdp_pe = epe.n
    dump("ctx", ctx_tm.rearrange("p c t -> p (c t)"), [(edve, v_sdp_dve)])

    barrier()

    tps67 = [psD_bf[:, 0:1024], psD_bf[:, 1024:2048]]
    tp_ev = {}
    for tc in range(NCH):
        tp = tps67[tc % 2]
        w = [(edve, v_sdp_dve)] if tc < 2 else [(edve, tp_ev[tc - 2])]
        vp = None
        for fc in range(FCH):
            vp = P.op("pe", lambda e, tc=tc, fc=fc, tp=tp: e.transpose(
                tp[:, fc * 128:(fc + 1) * 128],
                ctx_tm[:, tc, fc * 128:(fc + 1) * 128], I128[:, :]),
                waits=w if fc == 0 else (), inc=epe)
        vv = P.op("dve", lambda e, tc=tc, tp=tp: e.tensor_copy(
            ctx_fm[:, :, tc * 128:(tc + 1) * 128],
            tp.rearrange("p (c t) -> p c t", c=FCH)),
            waits=[(epe, vp)], inc=edve)
        tp_ev[tc] = vv
    v_ctxfm = edve.n

    pr3 = [psA, psB, psC]
    pr_free = {}
    slab_free = {}
    wo_si = [0]
    wo_ri = [0]
    wo_ev = {}

    def wo_step(m, th, regs=(0, 1, 2)):
        si = wo_si[0]
        wo_si[0] += 1
        sl = si % 2
        wsl = [(epe, slab_free.get(si - 2))] if si >= 2 else \
              [(epe, v_sdp_pe)]
        vd = P.op("pool", lambda e, m=m, sl=sl: e.dma_start(
            wstage[:, sl, :], wo_d[m]), waits=wsl, inc=qw[sl], inc_k=16)
        r = regs[wo_ri[0] % len(regs)]
        wo_ri[0] += 1
        w = [(qw[sl], vd), (edve, tp_ev[7] if th == 0 else v_ctxfm),
             pr_free.get(r)]
        if m == 0 and th == 0:
            w.append((eact, v_sdp_act))
        vp = None
        for k in range(FCH):
            for nh in range(2):
                vp = P.op("pe", lambda e, sl=sl, k=k, th=th, nh=nh,
                          r=r: e.matmul(
                    pr3[r][:, nh * 512:(nh + 1) * 512],
                    wstage[:, sl, k * 128:(k + 1) * 128],
                    ctx_fm[:, k, th * 1024 + nh * 512:
                           th * 1024 + (nh + 1) * 512],
                    start=(k == 0), stop=(k == FCH - 1)),
                    waits=w if (k == 0 and nh == 0) else (),
                    inc=epe if (k == FCH - 1 and nh == 1) else None)
        va = P.op("act", lambda e, m=m, th=th, r=r: e.activation(
            att_fm[:, m, th * 1024:(th + 1) * 1024], pr3[r][:, :],
            AF.Identity, bias=mc[:, C_BO + m:C_BO + m + 1], scale=1.0),
            waits=[(epe, vp)], inc=eact)
        pr_free[r] = (eact, va)
        wo_ev[(m, th)] = va
        slab_free[si] = vp

    for m in range(FCH):
        wo_step(m, 0)
    v_wo_th0 = eact.n

    mm = [psA, psB, psC]
    mm_free = {}
    wb_free = {0: (epe, v_sdp_pe), 1: (epe, v_sdp_pe)}
    out_st = {}
    sseq = [0]
    ri_b = [0]
    last_sA = [None]
    TS = {tt: {"h_wr": {}, "g_wr": {}, "g2_wr": {}, "ffn_wr": {},
               "epi_tp": {}, "outadd": {}, "aff4": {}, "r2e": {}}
          for tt in (0, 1)}

    def kloop(sl, coff, n_k, src_fn, w0, regs=(0, 1, 2)):
        r = regs[ri_b[0] % len(regs)]
        ri_b[0] += 1
        w = list(w0) + [mm_free.get(r)]
        vp = None
        for k in range(n_k):
            for nh in range(2):
                vp = P.op("pe", lambda e, sl=sl, coff=coff, k=k, nh=nh,
                          r=r, src_fn=src_fn: e.matmul(
                    mm[r][:, nh * 512:(nh + 1) * 512],
                    wbuf[:, sl, coff + k * 128:coff + (k + 1) * 128],
                    src_fn(k, nh), start=(k == 0), stop=(k == n_k - 1)),
                    waits=w if (k == 0 and nh == 0) else (),
                    inc=epe if (k == n_k - 1 and nh == 1) else None)
        return r, vp

    ln2s0 = lambda k, nh: ln2[:, k, nh * 512:(nh + 1) * 512]
    ln2s1 = lambda k, nh: ln2[:, k, 1024 + nh * 512:1024 + (nh + 1) * 512]
    hs = lambda k, nh: h_buf[:, k, nh * 512:(nh + 1) * 512]
    gs = lambda k, nh: g_buf[:, k, nh * 512:(nh + 1) * 512]

    def w12_step(tt, m, extra=(), regs=(0, 1, 2)):
        st = TS[tt]
        sl = sseq[0] % 2
        sseq[0] += 1
        vd = P.op("pool", lambda e, m=m, sl=sl: e.dma_start(
            wbuf[:, sl, 0:1024], w1_d[m]),
            waits=[wb_free[sl]], inc=qb[sl], inc_k=16)
        vd2 = P.op("pool", lambda e, m=m, sl=sl: e.dma_start(
            wbuf[:, sl, 1024:2048], w2_d[m]), inc=qb2[sl], inc_k=16)
        wA = [(qb[sl], vd)] + list(extra)
        ra, vpa = kloop(sl, 0, 8, ln2s0 if tt == 0 else ln2s1, wA, regs)
        rb_, vpb = kloop(sl, 1024, 8, ln2s0 if tt == 0 else ln2s1,
                         [(qb2[sl], vd2)], regs)
        wb_free[sl] = (epe, vpb)
        w = [(epe, vpa), (qc, v_konst), last_sA[0]]
        va = P.op("act", lambda e, m=m, ra=ra: e.activation(
            sA[:, :], mm[ra][:, :], AF.Silu,
            scale=mc[:, C_W1S + m:C_W1S + m + 1]), waits=w, inc=eact)
        mm_free[ra] = (eact, va)
        vh = P.op("dve", lambda e, m=m, rb_=rb_: e.scalar_tensor_tensor(
            h_buf[:, m, :], mm[rb_][:, :],
            mc[:, C_W2S + m:C_W2S + m + 1], sA[:, :],
            op0=ALU.mult, op1=ALU.mult),
            waits=[(epe, vpb), (eact, va)], inc=edve)
        mm_free[rb_] = (edve, vh)
        last_sA[0] = (edve, vh)
        st["h_wr"][m] = vh

    def big_step(tt, m, wd, src_fn, dst, scol, func, wr_key, extra=()):
        st = TS[tt]
        sl = sseq[0] % 2
        sseq[0] += 1
        vd = P.op("pool", lambda e, m=m, sl=sl, wd=wd: e.dma_start(
            wbuf[:, sl, :], wd[m]), waits=[wb_free[sl]],
            inc=qb[sl], inc_k=16)
        r, vp = kloop(sl, 0, 32, src_fn, [(qb[sl], vd)] + list(extra))
        wb_free[sl] = (epe, vp)
        va = P.op("act", lambda e, m=m, r=r, dst=dst, scol=scol, func=func:
                  e.activation(
            dst[:, m, :], mm[r][:, :], func,
            scale=mc[:, scol + m:scol + m + 1]),
            waits=[(epe, vp)], inc=eact)
        mm_free[r] = (eact, va)
        st[wr_key][m] = va

    def epi_chain(tt, t):
        st = TS[tt]
        v_ffn = (eact, st["ffn_wr"][7])
        tp_ev4 = st["epi_tp"]
        outadd = st["outadd"]
        aff4 = st["aff4"]
        gc = tt * 8 + t
        tp = tps67[t % 2]
        w = [v_ffn]
        if t >= 2:
            w.append((edve, lambda: tp_ev4[t - 2]))
        elif tt == 1:
            w.append((edve, TS[0]["epi_tp"][6 + t]))
        else:
            w.append((edve, tp_ev3.get(NCH - 2 + t)))
            w.append((edve, tp_ev2[NCH - 1]))
        vp = None
        for fc in range(FCH):
            vp = P.op("pe", lambda e, t=t, fc=fc, tp=tp: e.transpose(
                tp[:, fc * 128:(fc + 1) * 128],
                ffn_fm[:, fc, t * 128:(t + 1) * 128], I128[:, :]),
                waits=w if fc == 0 else (), inc=epe)
        yield
        w2_ = [(epe, vp)]
        if t >= 3:
            w2_.append((eact, lambda: aff4[t - 3][1]))
        elif tt == 1:
            w2_.append(TS[0]["aff4"][{0: 6, 1: 7, 2: 5}[t]])
        vv = P.op("dve", lambda e, t=t, tp=tp: e.tensor_copy(
            ffn_tm[:, t % 3, :], tp[:, 0:1024]), waits=w2_, inc=edve)
        tp_ev4[t] = vv
        yield
        w = [(qst[gc % 4], x1st[gc])]
        if t >= 3:
            w.append((edve, lambda: outadd[t - 3]))
        else:
            w.append((eact, st["g2_wr"][31]))
            w.append((edve, tp_ev2[NCH - 1]))
            w.append((edve, v_ln2))
            if tt == 1:
                w.append((edve, TS[0]["outadd"][{0: 6, 1: 7, 2: 5}[t]]))
        vx = P.op("sp", lambda e, gc=gc, t=t: e.dma_start(
            x1rel[:, t % 3, :], x1_d[gc * 128:(gc + 1) * 128, :]),
            waits=w, inc=qx[t % 3], inc_k=16)
        yield
        out = {}
        yield from ln_stats_g(ffn_tm[:, t % 3, :], C_ST + 16 * (t % 4),
                              sq_s[:, t % 2, :], [(edve, vv)], out,
                              war=((edve, lambda t=t, st=st: st["r2e"][t - 2])
                                   if t >= 2 else None))
        st["r2e"][t] = out["r2"]
        nv, cols = out["nv"], out["cols"]
        w = [(edve, nv)]
        if t >= 2:
            w.append((edve, lambda: outadd[t - 2]))
        elif tt == 1:
            w.append((edve, TS[0]["outadd"][6 + t]))
        elif tt == 0:
            w.append((edve, v_ln2))
        va = P.op("act", lambda e, t=t, cols=cols: e.activation(
            ln4t[:, t % 2, :], ffn_tm[:, t % 3, :], AF.Identity,
            bias=cols["nmr"], scale=cols["rstd"]), waits=w, inc=eact)
        aff4[t] = (eact, va)
        yield
        w = [(eact, va), (qx[t % 3], vx)]
        if gc >= 2:
            w.append((qo[gc % 2], lambda gc=gc: out_st[gc - 2][1]))
        if tt == 0 and t < 2:
            w += v_x1st
        vadd = P.op("dve", lambda e, t=t: e.tensor_add(
            xbuf[:, t % 2, :], x1rel[:, t % 3, :], ln4t[:, t % 2, :]),
            waits=w, inc=edve)
        outadd[t] = vadd
        yield
        vst = P.op("act", lambda e, gc=gc, t=t: e.dma_start(
            out_d[gc * 128:(gc + 1) * 128, :], xbuf[:, t % 2, :]),
            waits=[(edve, vadd)], inc=qo[gc % 2], inc_k=16)
        out_st[gc] = (qo[gc % 2], vst)

    def epi_loop(tt, weave=None):
        gens = [epi_chain(tt, t) for t in range(8)]
        drive(gens, 3)

    x0load = {}
    x1st = {}
    aff2 = {}
    tp_ev2 = {}
    tp_ev3 = {}
    x1add = {}
    ln2nv = {}
    r2l3 = {}
    r2l2 = {}
    tpsC = [psC_bf[:, 0:1024], psC_bf[:, 1024:2048]]
    v_wo_act = [None]

    def a3_chain(c):
        tp = tps67[c % 2]
        w = [(eact, (lambda: v_wo_act[0]) if c >= 8 else v_wo_th0)]
        if c >= 2:
            w.append((edve, lambda c=c: tp_ev2[c - 2]))
        else:
            w.append((edve, tp_ev.get(NCH - 2 + c)))
        vp = None
        for fc in range(FCH):
            vp = P.op("pe", lambda e, c=c, fc=fc, tp=tp: e.transpose(
                tp[:, fc * 128:(fc + 1) * 128],
                att_fm[:, fc, c * 128:(c + 1) * 128], I128[:, :]),
                waits=w if fc == 0 else (), inc=epe)
        yield
        w2_ = [(epe, vp)]
        if c >= 4:
            w2_.append((eact, lambda c=c: aff2[c - 4]))
        vv = P.op("dve", lambda e, c=c, tp=tp: e.tensor_copy(
            a3tm[:, c % 4, :], tp[:, 0:1024]), waits=w2_, inc=edve)
        tp_ev2[c] = vv
        yield
        out = {}
        yield from ln_stats_g(a3tm[:, c % 4, :], C_ST + 16 * (c % 4),
                              a3sq[:, c % 2, :], [(edve, vv)], out,
                              war=((edve, lambda c=c: r2l3[c - 2])
                                   if c >= 2 else None))
        r2l3[c] = out["r2"]
        nv, cols = out["nv"], out["cols"]
        w = []
        if c >= 3:
            w += [(eact, lambda c=c: aff2[c - 3]),
                  (edve, lambda c=c: ln2nv[c - 3]),
                  (qst[(c - 3) % 4], lambda c=c: x1st[c - 3])]
        else:
            w.append((edve, v_ln1))
        vx = P.op("sp", lambda e, c=c: e.dma_start(
            xbuf[:, c % 3, :], x_d[c * 128:(c + 1) * 128, :]),
            waits=w, inc=qx[c % 3], inc_k=16)
        x0load[c] = vx
        yield
        w = [(edve, nv)]
        if c >= 2:
            w.append((edve, lambda c=c: x1add[c - 2]))
        va = P.op("act", lambda e, c=c, cols=cols: e.activation(
            a3ln[:, c % 2, :], a3tm[:, c % 4, :], AF.Identity,
            bias=cols["nmr"], scale=cols["rstd"]), waits=w, inc=eact)
        yield
        vadd = P.op("dve", lambda e, c=c: e.tensor_add(
            xbuf[:, c % 3, :], xbuf[:, c % 3, :], a3ln[:, c % 2, :]),
            waits=[(eact, va), (qx[c % 3], x0load[c])], inc=edve)
        x1add[c] = vadd
        yield
        vst = P.op("act", lambda e, c=c: e.dma_start(
            x1_d[c * 128:(c + 1) * 128, :], xbuf[:, c % 3, :]),
            waits=[(edve, vadd)] +
                  ([(qst[c % 4], lambda c=c: x1st[c - 4])] if c >= 4 else []),
            inc=qst[c % 4], inc_k=16)
        x1st[c] = vst
        yield
        out = {}
        yield from ln_stats_g(xbuf[:, c % 3, :], C_ST + 64 + 16 * (c % 4),
                              a3sq[:, 2 + c % 2, :], [(edve, vadd)], out,
                              war=((edve, lambda c=c: r2l2[c - 2])
                                   if c >= 2 else None))
        r2l2[c] = out["r2"]
        nv2, cols2 = out["nv"], out["cols"]
        ln2nv[c] = nv2
        va2 = P.op("act", lambda e, c=c, cols2=cols2: e.activation(
            a3tm[:, c % 4, :], xbuf[:, c % 3, :], AF.Identity,
            bias=cols2["nmr"], scale=cols2["rstd"]),
            waits=[(edve, nv2)], inc=eact)
        aff2[c] = va2
        yield
        tp = tpsC[c % 2]
        w = [(eact, va2)]
        if c >= 2:
            w.append((edve, lambda c=c: tp_ev3[c - 2]))
        else:
            w.append((edve, v_sdp_dve))
        vp = None
        for fc in range(FCH):
            vp = P.op("pe", lambda e, c=c, fc=fc, tp=tp: e.transpose(
                tp[:, fc * 128:(fc + 1) * 128],
                a3tm[:, c % 4, fc * 128:(fc + 1) * 128], I128[:, :]),
                waits=w if fc == 0 else (), inc=epe)
        yield
        vv3 = P.op("dve", lambda e, c=c, tp=tp: e.tensor_copy(
            ln2[:, :, c * 128:(c + 1) * 128],
            tp.rearrange("p (c t) -> p c t", c=FCH)),
            waits=[(epe, vp)], inc=edve)
        tp_ev3[c] = vv3

    def wo1_shot(m):
        def g():
            wo_step(m, 1, regs=(0, 1))
            if m == FCH - 1:
                v_wo_act[0] = eact.n
                dump("att", att_fm.rearrange("p c t -> p (c t)"),
                     [(eact, v_wo_act[0])])
            yield
        return g()

    gen_list = []
    for c in range(NCH):
        gen_list.append(a3_chain(c))
        if c < FCH:
            gen_list.append(wo1_shot(c))
    drive(gen_list, 3)
    v_ln2 = edve.n
    v_x1st = [(s, s.n) for s in qst]
    mm_free[2] = (edve, v_ln2)
    dump("ln2", ln2.rearrange("p c t -> p (c t)"), [(edve, v_ln2)])

    barrier()

    for m in range(32):
        w12_step(0, m, extra=[(edve, tp_ev3[7])])
    dump("h", h_buf.rearrange("p c t -> p (c t)"), [(edve, TS[0]["h_wr"][31])])
    for m in range(32):
        big_step(0, m, h1_d, hs, g_buf, C_H1S, AF.Gelu, "g_wr",
                 extra=[(edve, TS[0]["h_wr"][31])] if m == 0 else ())
    dump("g", g_buf.rearrange("p c t -> p (c t)"), [(eact, TS[0]["g_wr"][31])])
    for m in range(32):
        big_step(0, m, h2_d, gs, h_buf, C_H2S, AF.Gelu, "g2_wr",
                 extra=[(eact, TS[0]["g_wr"][31]),
                        (edve, TS[0]["h_wr"][31])] if m == 0 else ())
    for m in range(8):
        big_step(0, m, w3_d, hs, ffn_fm, C_W3S, AF.Copy, "ffn_wr",
                 extra=[(eact, TS[0]["g2_wr"][31])] if m == 0 else ())
    dump("ffn", ffn_fm.rearrange("p c t -> p (c t)"),
         [(eact, TS[0]["ffn_wr"][7])])

    epi_loop(0)
    for m in range(32):
        w12_step(1, m, extra=[(edve, v_ln2)])
    for m in range(32):
        ex = []
        if m == 0:
            ex = [(edve, TS[1]["h_wr"][31])]
        elif m == 8:
            ex = [(edve, TS[0]["outadd"].get(7, 0)),
                  (eact, TS[0]["aff4"].get(7, (eact, 0))[1])]
        big_step(1, m, h1_d, hs, g_buf, C_H1S, AF.Gelu, "g_wr", extra=ex)
    for m in range(32):
        big_step(1, m, h2_d, gs, h_buf, C_H2S, AF.Gelu, "g2_wr",
                 extra=[(eact, TS[1]["g_wr"][31]),
                        (edve, TS[1]["h_wr"][31])] if m == 0 else ())
    for m in range(8):
        big_step(1, m, w3_d, hs, ffn_fm, C_W3S, AF.Copy, "ffn_wr",
                 extra=[(eact, TS[1]["g2_wr"][31]),
                        (edve, TS[0]["epi_tp"][7])] if m == 0 else ())
    epi_loop(1)

    P.wait("sp", qo[0], qo[0].n)
    P.wait("sp", qo[1], qo[1].n)

    with nc.Block() as block:
        @block.tensor
        def _(e):
            for f in P.q["pe"]:
                f(e)

        @block.scalar
        def _(e):
            for f in P.q["act"]:
                f(e)

        @block.vector
        def _(e):
            for f in P.q["dve"]:
                f(e)

        @block.gpsimd
        def _(e):
            for f in P.q["pool"]:
                f(e)

        @block.sync
        def _(e):
            for f in P.q["sp"]:
                f(e)

    cm.close()
    return nc


def _prep(inputs):
    g = lambda k: np.asarray(inputs[k], dtype=np.float32)

    def slab(w, mm, kk):
        return np.ascontiguousarray(
            w.reshape(mm, 128, kk, 128).transpose(0, 3, 2, 1).reshape(
                mm, 128, kk * 128))

    col = lambda v, n: np.ascontiguousarray(v.reshape(n, 128).T)
    shared = {
        "wq": slab(g("L0_attn_Wq"), 8, 8), "wk": slab(g("L0_attn_Wk"), 8, 8),
        "wo": slab(g("L0_attn_Wo"), 8, 8),
        "wv": np.ascontiguousarray(
            g("L0_attn_Wv").reshape(1024, 8, 128).transpose(1, 2, 0)),
        "w1": slab(g("L0_ffn_w1_w"), 32, 8), "w2": slab(g("L0_ffn_w2_w"), 32, 8),
        "h1": slab(g("L0_ffn_h1_w"), 32, 32), "h2": slab(g("L0_ffn_h2_w"), 32, 32),
        "w3": slab(g("L0_ffn_w3_w"), 8, 32),
        "bq": col(g("L0_attn_bq"), 8), "bk": col(g("L0_attn_bk"), 8),
        "bo": col(g("L0_attn_bo"), 8),
        "bv": np.ascontiguousarray(g("L0_attn_bv").reshape(1, 1024)),
        "sl": np.ascontiguousarray(g("L0_attn_slopes").reshape(1, 16)),
        "s1": col(g("L0_ffn_w1_s"), 32), "s2": col(g("L0_ffn_w2_s"), 32),
        "sh1": col(g("L0_ffn_h1_s"), 32), "sh2": col(g("L0_ffn_h2_s"), 32),
        "s3": col(g("L0_ffn_w3_s"), 8),
    }
    x = np.ascontiguousarray(np.asarray(inputs["L0_attn_x"], dtype=np.float32))
    return [dict(shared, x=np.ascontiguousarray(x[c])) for c in range(8)]


def kernel(**inputs):
    maps = _prep(inputs)
    nc = build_nc()
    res = run_bass_kernel_spmd(nc, maps, core_ids=list(range(8)))
    return np.stack([np.asarray(r["out"]) for r in res.results],
                    axis=0).astype(np.float32)
